# version 03
# baseline (speedup 1.0000x reference)
import contextlib
import numpy as np
import ml_dtypes
import concourse.bass as bass
import concourse.mybir as mybir
from concourse.bass_utils import run_bass_kernel_spmd

F32 = mybir.dt.float32
BF16 = mybir.dt.bfloat16
I8 = mybir.dt.int8
AF = mybir.ActivationFunctionType
ALU = mybir.AluOpType
AX = mybir.AxisListType

C = 1024
T = 2048
H = 16
HS = 64
PAIRS = 8
CCH = 8
SC = 16
QB = 1024
NQB = 2
EPS = 1e-5
MAGIC = float(np.float32(3 * 2.0 ** 22))

ENGINES = ("sync", "tensor", "scalar", "vector", "gpsimd")


class Sem:
    def __init__(self, handle, name, fifo=False, prefix=False):
        self.h = handle
        self.name = name
        self.v = 0
        self.fifo = fifo
        self.waw_ok = False
        self.prefix = prefix


class Plan:

    def __init__(self, sems):
        self.ops = {e: [] for e in ENGINES}
        self.bufs = {}
        self.known = {e: {} for e in ENGINES}
        self.sems = sems

    def _wait(self, eng, tok):
        sem, v = tok
        if getattr(sem, "prefix", False):
            v = sem.v
        if self.known[eng].get(sem.name, 0) >= v:
            return
        self.known[eng][sem.name] = v
        h = sem.h
        self.ops[eng].append(lambda e, h=h, v=v: e.wait_ge(h, v))

    def op(self, eng, build, reads=(), writes=(), sem=None, n=1, skip_self_raw=False):
        if sem is None:
            sem = self.sems[eng]
        for k in reads:
            b = self.bufs.get(k)
            if b and b["w"] is not None:
                ws, wv = b["w"]
                if not (skip_self_raw and ws is sem):
                    self._wait(eng, b["w"])
        for k in writes:
            b = self.bufs.get(k)
            if b:
                if b["w"] is not None:
                    ws, wv = b["w"]
                    if not (ws is sem and sem.waw_ok):
                        self._wait(eng, b["w"])
                for tok in b["r"]:
                    ts_, tv = tok
                    if not (ts_ is sem and sem.waw_ok):
                        self._wait(eng, tok)
        sem.v += n
        v2 = sem.v
        self.ops[eng].append(
            lambda e, b=build, h=sem.h, n=n: b(e).then_inc(h, n))
        tok = (sem, v2)
        for k in writes:
            self.bufs[k] = {"w": tok, "r": []}
        for k in reads:
            b = self.bufs.setdefault(k, {"w": None, "r": []})
            b["r"].append(tok)
        return tok


def build_nc():
    nc = bass.Bass("TRN2", target_bir_lowering=False)

    d_xT = nc.dram_tensor("xT", [C, T], F32, kind="ExternalInput").ap()
    d_wq1 = nc.dram_tensor("wq1", [C, C], F32, kind="ExternalInput").ap()
    d_wk1 = nc.dram_tensor("wk1", [C, C], F32, kind="ExternalInput").ap()
    d_wq2 = nc.dram_tensor("wq2", [C, C], F32, kind="ExternalInput").ap()
    d_wk2 = nc.dram_tensor("wk2", [C, C], F32, kind="ExternalInput").ap()
    d_wv = nc.dram_tensor("wv", [C, C], F32, kind="ExternalInput").ap()
    d_wt8 = nc.dram_tensor("wt8", [C, C], I8, kind="ExternalInput").ap()
    d_g8 = nc.dram_tensor("g8", [128, CCH], F32, kind="ExternalInput").ap()
    d_b8 = nc.dram_tensor("b8", [128, CCH], F32, kind="ExternalInput").ap()
    d_ws8 = nc.dram_tensor("ws8", [128, CCH], F32, kind="ExternalInput").ap()
    d_fb8 = nc.dram_tensor("fb8", [128, CCH], F32, kind="ExternalInput").ap()
    d_lam = nc.dram_tensor("lam", [1, 1], F32, kind="ExternalInput").ap()
    d_ident = nc.dram_tensor("ident", [128, 128], F32, kind="ExternalInput").ap()
    d_yT = nc.dram_tensor("yT", [C, T], F32, kind="ExternalOutput").ap()
    d_uscr = nc.dram_tensor("u_scr", [C, T], F32).ap()
    d_bcs = nc.dram_tensor("bc_scr", [1, 2048], F32).ap()
    d_ccin = nc.dram_tensor("cc_in", [1, 4], F32).ap()
    d_ccout = nc.dram_tensor("cc_out", [1, 4], F32, addr_space="Shared").ap()

    with contextlib.ExitStack() as stk:
        E = stk.enter_context
        xbf = E(nc.sbuf_tensor("xbf", [128, CCH * T], BF16))
        kb = E(nc.sbuf_tensor("kb", [128, PAIRS * T], BF16))
        wb = E(nc.sbuf_tensor("wb", [128, CCH * C], BF16))
        ldb = E(nc.sbuf_tensor("ldb", [128, 2 * T], F32))
        vaug = E(nc.sbuf_tensor("vaug", [128, H * SC * 65], BF16))
        qt = E(nc.sbuf_tensor("qt", [128, QB], BF16))
        et = E(nc.sbuf_tensor("et", [128, 2 * QB], BF16))
        ob = E(nc.sbuf_tensor("ob", [128, CCH * QB], F32))
        bcb = E(nc.sbuf_tensor("bcb", [128, 2048], F32))
        rows = E(nc.sbuf_tensor("rows", [128, 2048], F32))
        tmp2 = E(nc.sbuf_tensor("tmp2", [128, 1024], F32))
        sbb = E(nc.sbuf_tensor("sbb", [128, 1024], BF16))
        sqq = E(nc.sbuf_tensor("sqq", [128, 1024], BF16))
        onesb = E(nc.sbuf_tensor("onesb", [128, 1], BF16))
        epsT = E(nc.sbuf_tensor("epsT", [128, 1], F32))
        ident = E(nc.sbuf_tensor("idents", [128, 128], F32))
        g8 = E(nc.sbuf_tensor("g8s", [128, CCH], F32))
        b8 = E(nc.sbuf_tensor("b8s", [128, CCH], F32))
        ws8 = E(nc.sbuf_tensor("ws8s", [128, CCH], F32))
        fb8 = E(nc.sbuf_tensor("fb8s", [128, CCH], F32))
        gp8 = E(nc.sbuf_tensor("gp8", [128, CCH], F32))
        bp8 = E(nc.sbuf_tensor("bp8", [128, CCH], F32))
        scl8 = E(nc.sbuf_tensor("scl8", [128, CCH], F32))
        lamt = E(nc.sbuf_tensor("lamt", [1, 4], F32))
        lamv = E(nc.sbuf_tensor("lamv", [128, 4], F32))
        xsin = E(nc.sbuf_tensor("xsin", [128, 2], F32))
        mx = E(nc.sbuf_tensor("mx", [128, 1], F32))
        mxt = E(nc.sbuf_tensor("mxt", [128, 1], F32))
        xq = xbf
        wt8v = kb[:, 0:4096].bitcast(I8)
        utb = kb[:, 4096:12288].bitcast(F32)
        fob = ob[:, 0:4096]

        st = E(nc.psum_tensor("st", [128, 2048], F32))
        pAE = E(nc.psum_tensor("pAE", [65, 1024], F32))
        pAO = E(nc.psum_tensor("pAO", [65, 1024], F32))

        sSM = Sem(E(nc.semaphore("sSM")), "sSM", prefix=True)
        sWT = Sem(E(nc.semaphore("sWT")), "sWT", prefix=True)
        sL0 = Sem(E(nc.semaphore("sL0")), "sL0", prefix=True)
        sL1 = Sem(E(nc.semaphore("sL1")), "sL1", prefix=True)
        sUT = Sem(E(nc.semaphore("sUT")), "sUT", prefix=True)
        sBC = Sem(E(nc.semaphore("sBC")), "sBC", prefix=True)
        sST = Sem(E(nc.semaphore("sST")), "sST", prefix=True)
        sCI = Sem(E(nc.semaphore("sCI")), "sCI", prefix=True)
        sPE = Sem(E(nc.semaphore("sPE")), "sPE", fifo=True)
        sPE.waw_ok = True
        sACT = Sem(E(nc.semaphore("sACT")), "sACT", fifo=True)
        sDVE = Sem(E(nc.semaphore("sDVE")), "sDVE", fifo=True)
        sCC = Sem(E(nc.semaphore("sCC")), "sCC")

        P = Plan({"sync": sSM, "tensor": sPE, "scalar": sACT,
                  "vector": sDVE, "gpsimd": sST})
        sLhalf = (sL0, sL1)

        def slot(base, col, n=1024):
            return rows[base:base + 1, col * n:col * n + n]

        def dma(eng, out, in_, reads, writes, sem):
            P.op(eng, lambda e, o=out, i=in_: e.dma_start(out=o, in_=i),
                 reads=reads, writes=writes, sem=sem, n=16)

        def bcast(src_row, width, dst, dst_key, src_key):
            dma("gpsimd", d_bcs[0:1, 0:width], src_row, (src_key,), ("d_bcs",), sBC)
            dma("gpsimd", dst, d_bcs[0:1, 0:width].partition_broadcast(128),
                ("d_bcs",), (dst_key,), sBC)

        dma("sync", g8[:, :], d_g8, (), ("g8",), sSM)
        dma("sync", b8[:, :], d_b8, (), ("b8",), sSM)
        dma("sync", ws8[:, :], d_ws8, (), ("ws8",), sSM)
        dma("sync", fb8[:, :], d_fb8, (), ("fb8",), sSM)
        dma("sync", lamt[0:1, 0:1], d_lam, (), ("lamt",), sSM)
        dma("sync", ident[:, :], d_ident, (), ("ident",), sSM)
        P.op("vector", lambda e: e.memset(vaug[:, :], 1.0), writes=("vaug",))
        P.op("vector", lambda e: e.memset(onesb[:, :], 1.0), writes=("onesb",))
        P.op("vector", lambda e: e.memset(epsT[:, :], EPS), writes=("epsT",))
        P.op("vector", lambda e: e.memset(mx[:, :], 0.0), writes=("mx",))
        P.op("vector",
             lambda e: e.tensor_scalar(lamt[0:1, 1:2], lamt[0:1, 0:1], -1.0, 1.0,
                                       ALU.mult, ALU.add),
             reads=("lamt",), writes=("lamt",))
        P.op("vector",
             lambda e: e.tensor_scalar(lamt[0:1, 2:3], lamt[0:1, 0:1], -1.0, None,
                                       ALU.mult),
             reads=("lamt",), writes=("lamt",))
        P.op("vector", lambda e: e.memset(lamt[0:1, 3:4], 0.0),
             reads=("lamt",), writes=("lamt",))
        bcast(lamt[0:1, 0:4], 4, lamv[:, 0:4], "lamv", "lamt")
        P.op("vector",
             lambda e: e.tensor_scalar(gp8[:, :], g8[:, :], lamv[:, 1:2], None, ALU.mult),
             reads=("g8", "lamv"), writes=("gp8",))
        P.op("vector",
             lambda e: e.tensor_scalar(bp8[:, :], b8[:, :], lamv[:, 1:2], None, ALU.mult),
             reads=("b8", "lamv"), writes=("bp8",))

        for ci in range(CCH):
            half = ci % 2
            lslice = ldb[:, half * T:(half + 1) * T]
            lkey = f"ldb{half}"
            dma("sync", lslice, d_xT[ci * 128:(ci + 1) * 128, :], (), (lkey,),
                sLhalf[half])
            P.op("vector",
                 lambda e, ls=lslice, ci=ci: e.tensor_copy(
                     xbf[:, ci * T:(ci + 1) * T], ls),
                 reads=(lkey,), writes=(f"xbf{ci}",))

        def load_weight(dram):
            for ci in range(CCH):
                half = ci % 2
                lslice = ldb[:, half * T:half * T + C]
                lkey = f"ldb{half}"
                dma("sync", lslice, dram[ci * 128:(ci + 1) * 128, :], (), (lkey,),
                    sLhalf[half])
                P.op("vector",
                     lambda e, ls=lslice, ci=ci: e.tensor_copy(
                         wb[:, ci * C:(ci + 1) * C], ls),
                     reads=(lkey,), writes=(f"wb{ci}",))

        load_weight(d_wv)
        for p in range(PAIRS):
            for sc in range(SC):
                vps = st[:, 0:128]
                for ci in range(CCH):
                    P.op("tensor",
                         lambda e, ci=ci, sc=sc, p=p, vps=vps: e.matmul(
                             vps,
                             xbf[:, ci * T + sc * 128: ci * T + sc * 128 + 128],
                             wb[:, ci * C + p * 128: ci * C + p * 128 + 128],
                             start=(ci == 0), stop=(ci == CCH - 1)),
                         reads=(f"xbf{ci}", f"wb{ci}"),
                         writes=("st",),
                         sem=sPE)
                o1 = (2 * p * SC + sc) * 65
                o2 = ((2 * p + 1) * SC + sc) * 65
                P.op("vector",
                     lambda e, o1=o1, vps=vps: e.tensor_copy(
                         vaug[:, o1:o1 + 64], vps[:, 0:64]),
                     reads=("st",), writes=("vaug",))
                P.op("vector",
                     lambda e, o2=o2, vps=vps: e.tensor_copy(
                         vaug[:, o2:o2 + 64], vps[:, 64:128]),
                     reads=("st",), writes=("vaug",))

        def stats_rows(src, src_keys, n, ncch, col0):
            for ci in range(ncch):
                P.op("vector",
                     lambda e, ci=ci: e.tensor_copy(
                         sbb[:, 0:n], src[:, ci * n + col0: ci * n + col0 + n]),
                     reads=(src_keys[ci],), writes=("sbb",))
                P.op("vector",
                     lambda e: e.tensor_mul(sqq[:, 0:n], sbb[:, 0:n], sbb[:, 0:n]),
                     reads=("sbb",), writes=("sqq",))
                for j in range((n + 511) // 512):
                    w = min(512, n - j * 512)
                    P.op("tensor",
                         lambda e, j=j, w=w, ci=ci: e.matmul(
                             pAE[0:1, j * 512:j * 512 + w], onesb[:, 0:1],
                             sbb[:, j * 512:j * 512 + w],
                             start=(ci == 0), stop=(ci == ncch - 1)),
                         reads=("onesb", "sbb"),
                         writes=("pAE",),
                         sem=sPE)
                    P.op("tensor",
                         lambda e, j=j, w=w, ci=ci: e.matmul(
                             pAO[0:1, j * 512:j * 512 + w], onesb[:, 0:1],
                             sqq[:, j * 512:j * 512 + w],
                             start=(ci == 0), stop=(ci == ncch - 1)),
                         reads=("onesb", "sqq"),
                         writes=("pAO",),
                         sem=sPE)
            P.op("vector", lambda e: e.tensor_scalar(
                slot(0, 0, n), pAE[0:1, 0:n], 1.0 / C, None, ALU.mult),
                reads=("pAE",), writes=("rows",))
            P.op("vector", lambda e: e.tensor_scalar(
                slot(0, 1, n), pAO[0:1, 0:n], 1.0 / C, None, ALU.mult),
                reads=("pAO",), writes=("rows",))
            P.op("vector", lambda e: e.tensor_mul(
                slot(32, 0, n), slot(0, 0, n), slot(0, 0, n)),
                reads=("rows",), writes=("rows",))
            P.op("vector", lambda e: e.tensor_copy(
                slot(32, 1, n), slot(0, 1, n)),
                reads=("rows",), writes=("rows",))
            P.op("vector", lambda e: e.tensor_sub(
                slot(32, 1, n), slot(32, 1, n), slot(32, 0, n)),
                reads=("rows",), writes=("rows",))
            P.op("scalar", lambda e: e.activation(
                slot(32, 0, n), slot(32, 1, n), AF.Ln,
                bias=epsT[0:1, 0:1], scale=1.0),
                reads=("rows", "epsT"), writes=("rows",))
            P.op("scalar", lambda e: e.activation(
                slot(64, 0, n), slot(32, 0, n), AF.Exp, bias=0.0, scale=-0.5),
                reads=("rows",), writes=("rows",))
            P.op("vector", lambda e: e.tensor_copy(
                slot(64, 1, n), slot(0, 0, n)),
                reads=("rows",), writes=("rows",))
            P.op("vector", lambda e: e.tensor_mul(
                slot(64, 1, n), slot(64, 0, n), slot(64, 1, n)),
                reads=("rows",), writes=("rows",))
            bcast(rows[64:65, 0:2 * n], 2 * n, bcb[:, 0:2 * n], "bcb", "rows")

        for qb in range(NQB):
            q0 = qb * QB
            obkeys = [f"ob{ci}" for ci in range(CCH)]
            for br in range(2):
                load_weight(d_wk1 if br == 0 else d_wk2)
                for p in range(PAIRS):
                    for j in range(4):
                        for ci in range(CCH):
                            P.op("tensor",
                                 lambda e, j=j, ci=ci, p=p: e.matmul(
                                     st[:, j * 512:(j + 1) * 512],
                                     wb[:, ci * C + p * 128: ci * C + p * 128 + 128],
                                     xbf[:, ci * T + j * 512: ci * T + (j + 1) * 512],
                                     start=(ci == 0), stop=(ci == CCH - 1)),
                                 reads=(f"wb{ci}", f"xbf{ci}"),
                                 writes=("st",),
                                 sem=sPE)
                    P.op("vector",
                         lambda e, p=p: e.tensor_copy(
                             kb[:, p * T:(p + 1) * T], st[:, :]),
                         reads=("st",), writes=(f"kb{p}",))
                load_weight(d_wq1 if br == 0 else d_wq2)
                for p in range(PAIRS):
                    for j in range(2):
                        for ci in range(CCH):
                            P.op("tensor",
                                 lambda e, j=j, ci=ci, p=p, q0=q0: e.matmul(
                                     st[:, j * 512:(j + 1) * 512],
                                     wb[:, ci * C + p * 128: ci * C + p * 128 + 128],
                                     xbf[:, ci * T + q0 + j * 512:
                                         ci * T + q0 + (j + 1) * 512],
                                     start=(ci == 0), stop=(ci == CCH - 1)),
                                 reads=(f"wb{ci}", f"xbf{ci}"),
                                 writes=("st",),
                                 sem=sPE)
                    P.op("vector",
                         lambda e: e.tensor_copy(qt[:, :], st[:, 0:QB]),
                         reads=("st",), writes=("qt",))
                    for sc in range(SC):
                        ko = p * T + sc * 128
                        for j in range(2):
                            P.op("tensor",
                                 lambda e, j=j, ko=ko: e.matmul(
                                     st[:, j * 512:(j + 1) * 512],
                                     kb[0:64, ko:ko + 128],
                                     qt[0:64, j * 512:(j + 1) * 512],
                                     start=True, stop=True,
                                     tile_position=(0, 0)),
                                 reads=(f"kb{p}", "qt"),
                                 writes=("st",),
                                 sem=sPE)
                            P.op("tensor",
                                 lambda e, j=j, ko=ko: e.matmul(
                                     st[:, 1024 + j * 512:1024 + (j + 1) * 512],
                                     kb[64:128, ko:ko + 128],
                                     qt[64:128, j * 512:(j + 1) * 512],
                                     start=True, stop=True,
                                     tile_position=(64, 0)),
                                 reads=(f"kb{p}", "qt"),
                                 writes=("st",),
                                 sem=sPE)
                        P.op("scalar",
                             lambda e: e.activation(et[:, :], st[:, :], AF.Exp,
                                                    bias=0.0, scale=0.125),
                             reads=("st",), writes=("et",))
                        vE = (2 * p * SC + sc) * 65
                        vO = ((2 * p + 1) * SC + sc) * 65
                        for j in range(2):
                            for (vo, pA, pkey, c0) in (
                                    (vE, pAE, "pAE", 0), (vO, pAO, "pAO", 1024)):
                                P.op("tensor",
                                     lambda e, j=j, vo=vo, pA=pA, c0=c0, sc=sc: e.matmul(
                                         pA[:, j * 512:(j + 1) * 512],
                                         vaug[:, vo:vo + 65],
                                         et[:, c0 + j * 512:c0 + (j + 1) * 512],
                                         start=(sc == 0), stop=(sc == SC - 1)),
                                     reads=("vaug", "et"),
                                     writes=(pkey,),
                                     sem=sPE)
                    P.op("vector",
                         lambda e: e.reciprocal(rows[64:65, 0:1024], pAE[64:65, :]),
                         reads=("pAE",), writes=("rows",))
                    P.op("vector",
                         lambda e: e.reciprocal(rows[64:65, 1024:2048], pAO[64:65, :]),
                         reads=("pAO",), writes=("rows",))
                    bcast(rows[64:65, 0:2048], 2048, bcb[:, 0:2048], "bcb", "rows")
                    oc = p * QB
                    if br == 0:
                        P.op("vector",
                             lambda e, oc=oc: e.tensor_mul(
                                 ob[0:64, oc:oc + QB], pAE[0:64, :], bcb[0:64, 0:1024]),
                             reads=("pAE", "bcb"), writes=(obkeys[p],))
                        P.op("vector",
                             lambda e, oc=oc: e.tensor_mul(
                                 ob[64:128, oc:oc + QB], pAO[0:64, :],
                                 bcb[0:64, 1024:2048]),
                             reads=("pAO", "bcb"), writes=(obkeys[p],))
                    else:
                        P.op("vector",
                             lambda e: e.tensor_mul(
                                 tmp2[0:64, :], pAE[0:64, :], bcb[0:64, 0:1024]),
                             reads=("pAE", "bcb"), writes=("tmp2",))
                        P.op("vector",
                             lambda e: e.tensor_mul(
                                 tmp2[64:128, :], pAO[0:64, :], bcb[0:64, 1024:2048]),
                             reads=("pAO", "bcb"), writes=("tmp2",))
                        P.op("vector",
                             lambda e, oc=oc: e.scalar_tensor_tensor(
                                 ob[:, oc:oc + QB], tmp2[:, :], lamv[:, 2:3],
                                 ob[:, oc:oc + QB], ALU.mult, ALU.add),
                             reads=("tmp2", "lamv", obkeys[p]), writes=(obkeys[p],))

            stats_rows(ob, obkeys, QB, CCH, 0)
            for ci in range(CCH):
                half = ci % 2
                lslice = ldb[:, half * T:half * T + QB]
                lkey = f"ldb{half}"
                dma("sync", lslice, d_xT[ci * 128:(ci + 1) * 128, q0:q0 + QB],
                    (), (lkey,), sLhalf[half])
                oc = ci * QB
                P.op("vector",
                     lambda e, oc=oc: e.tensor_mul(
                         tmp2[:, :], ob[:, oc:oc + QB], bcb[:, 0:QB]),
                     reads=(obkeys[ci], "bcb"), writes=("tmp2",))
                P.op("vector",
                     lambda e: e.tensor_sub(tmp2[:, :], tmp2[:, :], bcb[:, QB:2 * QB]),
                     reads=("tmp2", "bcb"), writes=("tmp2",))
                P.op("vector",
                     lambda e, ci=ci: e.tensor_scalar(
                         tmp2[:, :], tmp2[:, :], gp8[:, ci:ci + 1],
                         bp8[:, ci:ci + 1], ALU.mult, ALU.add),
                     reads=("tmp2", "gp8", "bp8"), writes=("tmp2",))
                P.op("vector",
                     lambda e, oc=oc, ls=lslice: e.tensor_add(
                         ob[:, oc:oc + QB], tmp2[:, :], ls),
                     reads=("tmp2", lkey), writes=(obkeys[ci],))
            stats_rows(ob, obkeys, QB, CCH, 0)
            for ci in range(CCH):
                oc = ci * QB
                P.op("vector",
                     lambda e, oc=oc: e.tensor_mul(
                         tmp2[:, :], ob[:, oc:oc + QB], bcb[:, 0:QB]),
                     reads=(obkeys[ci], "bcb"), writes=("tmp2",))
                P.op("vector",
                     lambda e, oc=oc: e.tensor_sub(
                         ob[:, oc:oc + QB], tmp2[:, :], bcb[:, QB:2 * QB]),
                     reads=("tmp2", "bcb"), writes=(obkeys[ci],))
                P.op("vector",
                     lambda e, oc=oc: e.reduce_max(
                         mxt[:, 0:1], ob[:, oc:oc + QB], AX.X,
                         apply_absolute_value=True),
                     reads=(obkeys[ci],), writes=("mxt",))
                P.op("vector",
                     lambda e: e.tensor_max(mx[:, 0:1], mx[:, 0:1], mxt[:, 0:1]),
                     reads=("mxt", "mx"), writes=("mx",))
            dma("gpsimd",
                d_uscr[:, q0:q0 + QB].rearrange("(c p) q -> p c q", p=128),
                ob[:, :].rearrange("p (c q) -> p c q", c=CCH),
                tuple(obkeys), (f"uscr{qb}",), sST)

        P.op("tensor",
             lambda e: e.transpose(pAE[0:1, 0:128], mx[:, 0:1], ident[:, :]),
             reads=("mx", "ident"), writes=("pAE",), sem=sPE)
        P.op("vector", lambda e: e.memset(rows[0:1, 0:4], 0.0), writes=("rows",))
        P.op("vector",
             lambda e: e.reduce_max(rows[0:1, 0:1], pAE[0:1, 0:128], AX.X),
             reads=("pAE",), writes=("rows",))
        dma("gpsimd", d_ccin, rows[0:1, 0:4], ("rows",), ("d_ccin",), sCI)
        P.op("gpsimd",
             lambda e: e.collective_compute(
                 "AllReduce", ALU.max, replica_groups=[list(range(8))],
                 ins=[d_ccin], outs=[d_ccout]),
             reads=("d_ccin",), writes=("d_ccout",), sem=sCC, n=1)
        dma("gpsimd", rows[0:1, 4:8], d_ccout, ("d_ccout",), ("rows",), sCI)
        P.op("vector",
             lambda e: e.tensor_scalar(rows[0:1, 0:1], rows[0:1, 4:5],
                                       1.0 / 127.0, EPS, ALU.mult, ALU.max),
             reads=("rows",), writes=("rows",))
        P.op("vector",
             lambda e: e.reciprocal(rows[0:1, 1:2], rows[0:1, 0:1]),
             reads=("rows",), writes=("rows",))
        bcast(rows[0:1, 0:2], 2, xsin[:, 0:2], "xsin", "rows")
        P.op("vector",
             lambda e: e.tensor_scalar(scl8[:, :], ws8[:, :], xsin[:, 0:1],
                                       None, ALU.mult),
             reads=("ws8", "xsin"), writes=("scl8",))
        dma("sync", wt8v.rearrange("p (c n) -> p c n", c=CCH),
            d_wt8.rearrange("(c p) n -> p c n", p=128), (),
            ("wt8", "kb0", "kb1"), sWT)
        for ci in range(CCH):
            P.op("vector",
                 lambda e, ci=ci: e.tensor_copy(
                     wb[:, ci * C:(ci + 1) * C],
                     wt8v[:, ci * C:(ci + 1) * C]),
                 reads=("wt8",), writes=(f"wb{ci}",))
        for ci in range(CCH):
            half = ci % 2
            lslice = ldb[:, half * T:(half + 1) * T]
            lkey = f"ldb{half}"
            dma("sync", lslice, d_uscr[ci * 128:(ci + 1) * 128, :],
                ("uscr0", "uscr1"), (lkey,), sLhalf[half])
            P.op("vector",
                 lambda e, ls=lslice: e.tensor_scalar(
                     bcb[:, 0:2048], ls, xsin[:, 1:2], MAGIC, ALU.mult, ALU.add),
                 reads=(lkey, "xsin"), writes=("bcb",))
            P.op("vector",
                 lambda e, ci=ci: e.tensor_scalar(
                     xq[:, ci * T:(ci + 1) * T], bcb[:, 0:2048], MAGIC, None,
                     ALU.subtract),
                 reads=("bcb",), writes=(f"xq{ci}", f"xbf{ci}"))
        for tb in range(4):
            t0 = tb * 512
            dma("sync", utb.rearrange("p (c q) -> p c q", c=CCH),
                d_uscr[:, t0:t0 + 512].rearrange("(c p) q -> p c q", p=128),
                ("uscr0", "uscr1"),
                ("utb", "kb2", "kb3", "kb4", "kb5"), sUT)
            fkeys = [f"fob{nc_}" for nc_ in range(CCH)]
            for nc_ in range(CCH):
                fps = st[:, 0:512]
                fkey = "st"
                for ci in range(CCH):
                    P.op("tensor",
                         lambda e, ci=ci, nc_=nc_, fps=fps, t0=t0: e.matmul(
                             fps,
                             wb[:, ci * C + nc_ * 128: ci * C + nc_ * 128 + 128],
                             xq[:, ci * T + t0: ci * T + t0 + 512],
                             start=(ci == 0), stop=(ci == CCH - 1)),
                         reads=(f"wb{ci}", f"xq{ci}"),
                         writes=(fkey,),
                         sem=sPE)
                P.op("vector",
                     lambda e, nc_=nc_, fps=fps: e.tensor_scalar(
                         tmp2[:, 0:512], fps, scl8[:, nc_:nc_ + 1],
                         fb8[:, nc_:nc_ + 1], ALU.mult, ALU.add),
                     reads=(fkey, "scl8", "fb8"), writes=("tmp2",))
                P.op("vector",
                     lambda e, nc_=nc_: e.tensor_add(
                         fob[:, nc_ * 512:(nc_ + 1) * 512], tmp2[:, 0:512],
                         utb[:, nc_ * 512:(nc_ + 1) * 512]),
                     reads=("tmp2", "utb"),
                     writes=(fkeys[nc_], f"ob{nc_ // 2}"))
            stats_rows(fob, fkeys, 512, CCH, 0)
            for nc_ in range(CCH):
                oc = nc_ * 512
                P.op("vector",
                     lambda e, oc=oc: e.tensor_mul(
                         tmp2[:, 0:512], fob[:, oc:oc + 512], bcb[:, 0:512]),
                     reads=(fkeys[nc_], "bcb"), writes=("tmp2",))
                P.op("vector",
                     lambda e, oc=oc: e.tensor_sub(
                         fob[:, oc:oc + 512], tmp2[:, 0:512], bcb[:, 512:1024]),
                     reads=("tmp2", "bcb"), writes=(fkeys[nc_],))
            dma("gpsimd",
                d_yT[:, t0:t0 + 512].rearrange("(c p) q -> p c q", p=128),
                fob[:, :].rearrange("p (c q) -> p c q", c=CCH),
                tuple(fkeys), (f"y{tb}",), sST)

        v = sST.v
        P.ops["gpsimd"].append(lambda e, v=v: e.wait_ge(sST.h, v))

        with nc.Block() as block:
            @block.sync
            def _(eng):
                for f in P.ops["sync"]:
                    f(eng)

            @block.tensor
            def _(eng):
                for f in P.ops["tensor"]:
                    f(eng)

            @block.scalar
            def _(eng):
                for f in P.ops["scalar"]:
                    f(eng)

            @block.vector
            def _(eng):
                for f in P.ops["vector"]:
                    f(eng)

            @block.gpsimd
            def _(eng):
                for f in P.ops["gpsimd"]:
                    f(eng)

    return nc


_NC_CACHE = {}


def kernel(L0_attn_x, L0_attn_wq1, L0_attn_wk1, L0_attn_wq2, L0_attn_wk2,
           L0_attn_wv, L0_attn_ln_gamma, L0_attn_ln_beta, L0_attn_lamb,
           L0_ffn_weight_int8, L0_ffn_weight_scale, L0_ffn_bias):
    f32 = np.float32
    x = np.asarray(L0_attn_x, f32)
    B = x.shape[0]

    def headmajor(w):
        return np.ascontiguousarray(
            np.asarray(w, f32).transpose(1, 0, 2).reshape(C, C))

    wq1 = headmajor(L0_attn_wq1)
    wk1 = headmajor(L0_attn_wk1)
    wq2 = headmajor(L0_attn_wq2)
    wk2 = headmajor(L0_attn_wk2)
    wv = headmajor(L0_attn_wv)
    wt8 = np.ascontiguousarray(np.asarray(L0_ffn_weight_int8, np.int8).T)
    g8 = np.ascontiguousarray(np.asarray(L0_attn_ln_gamma, f32).reshape(CCH, 128).T)
    b8 = np.ascontiguousarray(np.asarray(L0_attn_ln_beta, f32).reshape(CCH, 128).T)
    ws8 = np.ascontiguousarray(
        np.asarray(L0_ffn_weight_scale, f32).reshape(CCH, 128).T)
    fb8 = np.ascontiguousarray(np.asarray(L0_ffn_bias, f32).reshape(CCH, 128).T)
    lam = np.asarray(L0_attn_lamb, f32).reshape(1, 1)
    ident = np.eye(128, dtype=f32)

    if "nc" not in _NC_CACHE:
        _NC_CACHE["nc"] = build_nc()
    nc = _NC_CACHE["nc"]

    in_maps = []
    for b in range(B):
        in_maps.append(dict(
            xT=np.ascontiguousarray(x[b].T), wq1=wq1, wk1=wk1, wq2=wq2,
            wk2=wk2, wv=wv, wt8=wt8, g8=g8, b8=b8, ws8=ws8, fb8=fb8,
            lam=lam, ident=ident))
    res = run_bass_kernel_spmd(nc, in_maps, core_ids=list(range(B)))
    out = np.stack([np.asarray(res.results[b]["yT"], f32).T for b in range(B)],
                   axis=0)
    return out


# version 04
# speedup vs baseline: 1.0602x; 1.0602x over previous
import contextlib
import numpy as np
import ml_dtypes
import concourse.bass as bass
import concourse.mybir as mybir
from concourse.bass_utils import run_bass_kernel_spmd

F32 = mybir.dt.float32
BF16 = mybir.dt.bfloat16
I8 = mybir.dt.int8
AF = mybir.ActivationFunctionType
ALU = mybir.AluOpType
AX = mybir.AxisListType

C = 1024
T = 2048
H = 16
HS = 64
PAIRS = 8
CCH = 8
SC = 16
QB = 1024
NQB = 2
EPS = 1e-5
MAGIC = float(np.float32(3 * 2.0 ** 22))
VW = 130

ENGINES = ("sync", "tensor", "scalar", "vector", "gpsimd")


class Sem:
    def __init__(self, handle, name, fifo=False, prefix=False):
        self.h = handle
        self.name = name
        self.v = 0
        self.fifo = fifo
        self.waw_ok = False
        self.prefix = prefix


class Plan:

    def __init__(self, sems):
        self.ops = {e: [] for e in ENGINES}
        self.bufs = {}
        self.known = {e: {} for e in ENGINES}
        self.sems = sems

    def _wait(self, eng, tok):
        sem, v = tok
        if getattr(sem, "prefix", False):
            v = sem.v
        if self.known[eng].get(sem.name, 0) >= v:
            return
        self.known[eng][sem.name] = v
        h = sem.h
        self.ops[eng].append(lambda e, h=h, v=v: e.wait_ge(h, v))

    def op(self, eng, build, reads=(), writes=(), sem=None, n=1, skip_self_raw=False):
        if sem is None:
            sem = self.sems[eng]
        for k in reads:
            b = self.bufs.get(k)
            if b and b["w"] is not None:
                ws, wv = b["w"]
                if not (skip_self_raw and ws is sem):
                    self._wait(eng, b["w"])
        for k in writes:
            b = self.bufs.get(k)
            if b:
                if b["w"] is not None:
                    ws, wv = b["w"]
                    if not (ws is sem and sem.waw_ok):
                        self._wait(eng, b["w"])
                for tok in b["r"]:
                    ts_, tv = tok
                    if not (ts_ is sem and sem.waw_ok):
                        self._wait(eng, tok)
        sem.v += n
        v2 = sem.v
        self.ops[eng].append(
            lambda e, b=build, h=sem.h, n=n: b(e).then_inc(h, n))
        tok = (sem, v2)
        for k in writes:
            self.bufs[k] = {"w": tok, "r": []}
        for k in reads:
            b = self.bufs.setdefault(k, {"w": None, "r": []})
            b["r"].append(tok)
        return tok


def build_nc():
    nc = bass.Bass("TRN2", target_bir_lowering=False)

    d_xT = nc.dram_tensor("xT", [C, T], F32, kind="ExternalInput").ap()
    d_wq1 = nc.dram_tensor("wq1", [C, C], F32, kind="ExternalInput").ap()
    d_wk1 = nc.dram_tensor("wk1", [C, C], F32, kind="ExternalInput").ap()
    d_wq2 = nc.dram_tensor("wq2", [C, C], F32, kind="ExternalInput").ap()
    d_wk2 = nc.dram_tensor("wk2", [C, C], F32, kind="ExternalInput").ap()
    d_wv = nc.dram_tensor("wv", [C, C], F32, kind="ExternalInput").ap()
    d_wt8 = nc.dram_tensor("wt8", [C, C], I8, kind="ExternalInput").ap()
    d_g8 = nc.dram_tensor("g8", [128, CCH], F32, kind="ExternalInput").ap()
    d_b8 = nc.dram_tensor("b8", [128, CCH], F32, kind="ExternalInput").ap()
    d_ws8 = nc.dram_tensor("ws8", [128, CCH], F32, kind="ExternalInput").ap()
    d_fb8 = nc.dram_tensor("fb8", [128, CCH], F32, kind="ExternalInput").ap()
    d_lam = nc.dram_tensor("lam", [1, 1], F32, kind="ExternalInput").ap()
    d_ident = nc.dram_tensor("ident", [128, 128], F32, kind="ExternalInput").ap()
    d_yT = nc.dram_tensor("yT", [C, T], F32, kind="ExternalOutput").ap()
    d_uscr = nc.dram_tensor("u_scr", [C, T], F32).ap()
    d_bcs = nc.dram_tensor("bc_scr", [1, 2048], F32).ap()
    d_ccin = nc.dram_tensor("cc_in", [1, 4], F32).ap()
    d_ccout = nc.dram_tensor("cc_out", [1, 4], F32, addr_space="Shared").ap()

    with contextlib.ExitStack() as stk:
        E = stk.enter_context
        xbf = E(nc.sbuf_tensor("xbf", [128, CCH * T], BF16))
        kb = E(nc.sbuf_tensor("kb", [128, PAIRS * T], BF16))
        wb = E(nc.sbuf_tensor("wb", [128, CCH * C], BF16))
        ldb = E(nc.sbuf_tensor("ldb", [128, 1024], F32))
        vaug = E(nc.sbuf_tensor("vaug", [128, PAIRS * SC * VW + 63], BF16))
        qt = E(nc.sbuf_tensor("qt", [128, QB], BF16))
        etb = E(nc.sbuf_tensor("etb", [128, 2 * QB], BF16))
        ob = E(nc.sbuf_tensor("ob", [128, CCH * QB], F32))
        eub = E(nc.sbuf_tensor("eub", [128, 2048], F32))
        bcb = E(nc.sbuf_tensor("bcb", [128, 2048], F32))
        rows = E(nc.sbuf_tensor("rows", [128, 2048], F32))
        tmp2 = E(nc.sbuf_tensor("tmp2", [128, 1024], F32))
        sbb = E(nc.sbuf_tensor("sbb", [128, 1024], BF16))
        sqq = E(nc.sbuf_tensor("sqq", [128, 1024], BF16))
        onesb = E(nc.sbuf_tensor("onesb", [128, 128], BF16))
        epsT = E(nc.sbuf_tensor("epsT", [128, 1], F32))
        ident = E(nc.sbuf_tensor("idents", [128, 128], F32))
        g8 = E(nc.sbuf_tensor("g8s", [128, CCH], F32))
        b8 = E(nc.sbuf_tensor("b8s", [128, CCH], F32))
        ws8 = E(nc.sbuf_tensor("ws8s", [128, CCH], F32))
        fb8 = E(nc.sbuf_tensor("fb8s", [128, CCH], F32))
        gp8 = E(nc.sbuf_tensor("gp8", [128, CCH], F32))
        bp8 = E(nc.sbuf_tensor("bp8", [128, CCH], F32))
        scl8 = E(nc.sbuf_tensor("scl8", [128, CCH], F32))
        lamt = E(nc.sbuf_tensor("lamt", [1, 4], F32))
        lamv = E(nc.sbuf_tensor("lamv", [128, 4], F32))
        xsin = E(nc.sbuf_tensor("xsin", [128, 2], F32))
        mx = E(nc.sbuf_tensor("mx", [128, 1], F32))
        mxt = E(nc.sbuf_tensor("mxt", [128, 1], F32))
        xq = xbf
        wt8v = kb[:, 0:4096].bitcast(I8)
        utb = kb[:, 4096:12288].bitcast(F32)
        fob = ob[:, 0:4096]

        st = E(nc.psum_tensor("st", [128, 2048], F32))
        pAE = E(nc.psum_tensor("pAE", [128, 1024], F32))
        pAO = E(nc.psum_tensor("pAO", [128, 1024], F32))

        sSM = Sem(E(nc.semaphore("sSM")), "sSM", prefix=True)
        sL0 = Sem(E(nc.semaphore("sL0")), "sL0", prefix=True)
        sWT = Sem(E(nc.semaphore("sWT")), "sWT", prefix=True)
        sUT = Sem(E(nc.semaphore("sUT")), "sUT", prefix=True)
        sBC = Sem(E(nc.semaphore("sBC")), "sBC", prefix=True)
        sST = Sem(E(nc.semaphore("sST")), "sST", prefix=True)
        sCI = Sem(E(nc.semaphore("sCI")), "sCI", prefix=True)
        sPE = Sem(E(nc.semaphore("sPE")), "sPE", fifo=True)
        sPE.waw_ok = True
        sACT = Sem(E(nc.semaphore("sACT")), "sACT", fifo=True)
        sDVE = Sem(E(nc.semaphore("sDVE")), "sDVE", fifo=True)
        sCC = Sem(E(nc.semaphore("sCC")), "sCC")

        P = Plan({"sync": sSM, "tensor": sPE, "scalar": sACT,
                  "vector": sDVE, "gpsimd": sST})

        def slot(base, col, n=1024):
            return rows[base:base + 1, col * n:col * n + n]

        def dma(eng, out, in_, reads, writes, sem):
            P.op(eng, lambda e, o=out, i=in_: e.dma_start(out=o, in_=i),
                 reads=reads, writes=writes, sem=sem, n=16)

        def bcast(src_row, width, dst, dst_key, src_key):
            dma("gpsimd", d_bcs[0:1, 0:width], src_row, (src_key,), ("d_bcs",), sBC)
            dma("gpsimd", dst, d_bcs[0:1, 0:width].partition_broadcast(128),
                ("d_bcs",), (dst_key,), sBC)

        dma("sync", g8[:, :], d_g8, (), ("g8",), sSM)
        dma("sync", b8[:, :], d_b8, (), ("b8",), sSM)
        dma("sync", ws8[:, :], d_ws8, (), ("ws8",), sSM)
        dma("sync", fb8[:, :], d_fb8, (), ("fb8",), sSM)
        dma("sync", lamt[0:1, 0:1], d_lam, (), ("lamt",), sSM)
        dma("sync", ident[:, :], d_ident, (), ("ident",), sSM)
        P.op("vector", lambda e: e.memset(vaug[:, :], 1.0), writes=("vaug",))
        P.op("vector", lambda e: e.memset(onesb[:, :], 1.0), writes=("onesb",))
        P.op("vector", lambda e: e.memset(epsT[:, :], EPS), writes=("epsT",))
        P.op("vector", lambda e: e.memset(mx[:, :], 0.0), writes=("mx",))
        P.op("vector",
             lambda e: e.tensor_scalar(lamt[0:1, 1:2], lamt[0:1, 0:1], -1.0, 1.0,
                                       ALU.mult, ALU.add),
             reads=("lamt",), writes=("lamt",))
        P.op("vector",
             lambda e: e.tensor_scalar(lamt[0:1, 2:3], lamt[0:1, 0:1], -1.0, None,
                                       ALU.mult),
             reads=("lamt",), writes=("lamt",))
        P.op("vector", lambda e: e.memset(lamt[0:1, 3:4], 0.0),
             reads=("lamt",), writes=("lamt",))
        bcast(lamt[0:1, 0:4], 4, lamv[:, 0:4], "lamv", "lamt")
        P.op("vector",
             lambda e: e.tensor_scalar(gp8[:, :], g8[:, :], lamv[:, 1:2], None, ALU.mult),
             reads=("g8", "lamv"), writes=("gp8",))
        P.op("vector",
             lambda e: e.tensor_scalar(bp8[:, :], b8[:, :], lamv[:, 1:2], None, ALU.mult),
             reads=("b8", "lamv"), writes=("bp8",))

        for ci in range(CCH):
            for jh in range(2):
                dma("sync", ldb[:, :],
                    d_xT[ci * 128:(ci + 1) * 128, jh * 1024:(jh + 1) * 1024],
                    ("ldb",), ("ldb",), sL0)
                P.op("vector",
                     lambda e, ci=ci, jh=jh: e.tensor_copy(
                         xbf[:, ci * T + jh * 1024: ci * T + (jh + 1) * 1024],
                         ldb[:, :]),
                     reads=("ldb",), writes=(f"xbf{ci}",))

        def load_weight(dram):
            for ci in range(CCH):
                dma("sync", ldb[:, :], dram[ci * 128:(ci + 1) * 128, :],
                    ("ldb",), ("ldb",), sL0)
                P.op("vector",
                     lambda e, ci=ci: e.tensor_copy(
                         wb[:, ci * C:(ci + 1) * C], ldb[:, :]),
                     reads=("ldb",), writes=(f"wb{ci}",))

        load_weight(d_wv)
        for p in range(PAIRS):
            for ci in range(CCH):
                for j in range(4):
                    P.op("tensor",
                         lambda e, ci=ci, j=j, p=p: e.matmul(
                             st[:, j * 512:(j + 1) * 512],
                             wb[:, ci * C + p * 128: ci * C + p * 128 + 128],
                             xbf[:, ci * T + j * 512: ci * T + (j + 1) * 512],
                             start=(ci == 0), stop=(ci == CCH - 1)),
                         reads=(f"wb{ci}", f"xbf{ci}"),
                         writes=("st0", "st1"),
                         sem=sPE)
            P.op("vector", lambda e: e.tensor_copy(eub[:, :], st[:, :]),
                 reads=("st0", "st1"), writes=("eub",))
            for k in range(SC):
                pA, pkey = (pAE, "pAE") if (k % 2 == 0) else (pAO, "pAO")
                P.op("tensor",
                     lambda e, k=k, pA=pA: e.transpose(
                         pA[:, 0:128], eub[:, k * 128:(k + 1) * 128], ident[:, :]),
                     reads=("eub", "ident"), writes=(pkey,), sem=sPE)
                vo = (p * SC + k) * VW
                P.op("vector",
                     lambda e, vo=vo, pA=pA: e.tensor_copy(
                         vaug[:, vo:vo + 64], pA[:, 0:64]),
                     reads=(pkey,), writes=("vaug",))
                P.op("vector",
                     lambda e, vo=vo, pA=pA: e.tensor_copy(
                         vaug[:, vo + 65:vo + 129], pA[:, 64:128]),
                     reads=(pkey,), writes=("vaug",))

        def stats_rows(src, src_keys, n, ncch, col0):
            nj = (n + 511) // 512
            for ci in range(ncch):
                P.op("vector",
                     lambda e, ci=ci: e.tensor_copy(
                         sbb[:, 0:n], src[:, ci * n + col0: ci * n + col0 + n]),
                     reads=(src_keys[ci],), writes=("sbb",))
                P.op("vector",
                     lambda e: e.tensor_mul(sqq[:, 0:n], sbb[:, 0:n], sbb[:, 0:n]),
                     reads=("sbb",), writes=("sqq",))
                for j in range(nj):
                    w = min(512, n - j * 512)
                    P.op("tensor",
                         lambda e, j=j, w=w, ci=ci: e.matmul(
                             pAE[:, j * 512:j * 512 + w], onesb[:, :],
                             sbb[:, j * 512:j * 512 + w],
                             start=(ci == 0), stop=(ci == ncch - 1)),
                         reads=("onesb", "sbb"),
                         writes=("pAE",),
                         sem=sPE)
                    P.op("tensor",
                         lambda e, j=j, w=w, ci=ci: e.matmul(
                             pAO[:, j * 512:j * 512 + w], onesb[:, :],
                             sqq[:, j * 512:j * 512 + w],
                             start=(ci == 0), stop=(ci == ncch - 1)),
                         reads=("onesb", "sqq"),
                         writes=("pAO",),
                         sem=sPE)
            P.op("vector", lambda e: e.tensor_scalar(
                slot(0, 0, n), pAE[0:1, 0:n], 1.0 / C, None, ALU.mult),
                reads=("pAE",), writes=("rows",))
            P.op("vector", lambda e: e.tensor_scalar(
                slot(0, 1, n), pAO[0:1, 0:n], 1.0 / C, None, ALU.mult),
                reads=("pAO",), writes=("rows",))
            P.op("vector", lambda e: e.tensor_mul(
                slot(32, 0, n), slot(0, 0, n), slot(0, 0, n)),
                reads=("rows",), writes=("rows",))
            P.op("vector", lambda e: e.tensor_copy(
                slot(32, 1, n), slot(0, 1, n)),
                reads=("rows",), writes=("rows",))
            P.op("vector", lambda e: e.tensor_sub(
                slot(32, 1, n), slot(32, 1, n), slot(32, 0, n)),
                reads=("rows",), writes=("rows",))
            P.op("scalar", lambda e: e.activation(
                slot(32, 0, n), slot(32, 1, n), AF.Ln,
                bias=epsT[0:1, 0:1], scale=1.0),
                reads=("rows", "epsT"), writes=("rows",))
            P.op("scalar", lambda e: e.activation(
                slot(64, 0, n), slot(32, 0, n), AF.Exp, bias=0.0, scale=-0.5),
                reads=("rows",), writes=("rows",))
            P.op("vector", lambda e: e.tensor_copy(
                slot(64, 1, n), slot(0, 0, n)),
                reads=("rows",), writes=("rows",))
            P.op("vector", lambda e: e.tensor_mul(
                slot(64, 1, n), slot(64, 0, n), slot(64, 1, n)),
                reads=("rows",), writes=("rows",))
            bcast(rows[64:65, 0:2 * n], 2 * n, bcb[:, 0:2 * n], "bcb", "rows")

        for qb in range(NQB):
            q0 = qb * QB
            obkeys = [f"ob{ci}" for ci in range(CCH)]
            for br in range(2):
                load_weight(d_wk1 if br == 0 else d_wk2)
                for p in range(PAIRS):
                    for ci in range(CCH):
                        for j in range(4):
                            P.op("tensor",
                                 lambda e, j=j, ci=ci, p=p: e.matmul(
                                     st[:, j * 512:(j + 1) * 512],
                                     wb[:, ci * C + p * 128: ci * C + p * 128 + 128],
                                     xbf[:, ci * T + j * 512: ci * T + (j + 1) * 512],
                                     start=(ci == 0), stop=(ci == CCH - 1)),
                                 reads=(f"wb{ci}", f"xbf{ci}"),
                                 writes=("st0", "st1"),
                                 sem=sPE)
                    P.op("vector",
                         lambda e, p=p: e.tensor_copy(
                             kb[:, p * T:(p + 1) * T], st[:, :]),
                         reads=("st0", "st1"), writes=(f"kb{p}",))
                load_weight(d_wq1 if br == 0 else d_wq2)
                for p in range(PAIRS):
                    for ci in range(CCH):
                        for j in range(2):
                            P.op("tensor",
                                 lambda e, j=j, ci=ci, p=p, q0=q0: e.matmul(
                                     st[:, j * 512:(j + 1) * 512],
                                     wb[:, ci * C + p * 128: ci * C + p * 128 + 128],
                                     xbf[:, ci * T + q0 + j * 512:
                                         ci * T + q0 + (j + 1) * 512],
                                     start=(ci == 0), stop=(ci == CCH - 1)),
                                 reads=(f"wb{ci}", f"xbf{ci}"),
                                 writes=("st0",),
                                 sem=sPE)
                    P.op("vector",
                         lambda e: e.tensor_copy(qt[:, :], st[:, 0:QB]),
                         reads=("st0",), writes=("qt",))
                    for sc in range(SC):
                        ko = p * T + sc * 128
                        vo = (p * SC + sc) * VW
                        for j in range(2):
                            P.op("tensor",
                                 lambda e, j=j, ko=ko: e.matmul(
                                     st[:, j * 512:(j + 1) * 512],
                                     kb[0:64, ko:ko + 128],
                                     qt[0:64, j * 512:(j + 1) * 512],
                                     start=True, stop=True,
                                     tile_position=(0, 0)),
                                 reads=(f"kb{p}", "qt"),
                                 writes=("st0",),
                                 sem=sPE)
                        P.op("scalar",
                             lambda e: e.activation(etb[:, 0:1024], st[:, 0:1024],
                                                    AF.Exp, bias=0.0, scale=0.125),
                             reads=("st0",), writes=("etE",))
                        for j in range(2):
                            P.op("tensor",
                                 lambda e, j=j, ko=ko: e.matmul(
                                     st[:, 1024 + j * 512:1024 + (j + 1) * 512],
                                     kb[64:128, ko:ko + 128],
                                     qt[64:128, j * 512:(j + 1) * 512],
                                     start=True, stop=True,
                                     tile_position=(64, 0)),
                                 reads=(f"kb{p}", "qt"),
                                 writes=("st1",),
                                 sem=sPE)
                        P.op("scalar",
                             lambda e: e.activation(etb[:, 1024:2048],
                                                    st[:, 1024:2048],
                                                    AF.Exp, bias=0.0, scale=0.125),
                             reads=("st1",), writes=("etO",))
                        for j in range(2):
                            P.op("tensor",
                                 lambda e, j=j, vo=vo, sc=sc: e.matmul(
                                     pAE[:, j * 512:(j + 1) * 512],
                                     vaug[:, vo:vo + 128],
                                     etb[:, j * 512:(j + 1) * 512],
                                     start=(sc == 0), stop=(sc == SC - 1)),
                                 reads=("vaug", "etE"),
                                 writes=("pAE",),
                                 sem=sPE)
                        for j in range(2):
                            P.op("tensor",
                                 lambda e, j=j, vo=vo, sc=sc: e.matmul(
                                     pAO[:, j * 512:(j + 1) * 512],
                                     vaug[:, vo + 65:vo + 193],
                                     etb[:, 1024 + j * 512:1024 + (j + 1) * 512],
                                     start=(sc == 0), stop=(sc == SC - 1)),
                                 reads=("vaug", "etO"),
                                 writes=("pAO",),
                                 sem=sPE)
                    P.op("vector",
                         lambda e: e.tensor_copy(eub[0:65, 0:1024], pAE[0:65, :]),
                         reads=("pAE",), writes=("eub",))
                    P.op("vector",
                         lambda e: e.tensor_copy(eub[0:65, 1024:2048], pAO[0:65, :]),
                         reads=("pAO",), writes=("eub",))
                    P.op("vector",
                         lambda e: e.reciprocal(rows[64:65, 0:2048],
                                                eub[64:65, 0:2048]),
                         reads=("eub",), writes=("rows",))
                    bcast(rows[64:65, 0:2048], 2048, bcb[:, 0:2048], "bcb", "rows")
                    oc = p * QB
                    if br == 0:
                        P.op("vector",
                             lambda e, oc=oc: e.tensor_mul(
                                 ob[0:64, oc:oc + QB], eub[0:64, 0:1024],
                                 bcb[0:64, 0:1024]),
                             reads=("eub", "bcb"), writes=(obkeys[p],))
                        P.op("vector",
                             lambda e, oc=oc: e.tensor_mul(
                                 ob[64:128, oc:oc + QB], eub[0:64, 1024:2048],
                                 bcb[0:64, 1024:2048]),
                             reads=("eub", "bcb"), writes=(obkeys[p],))
                    else:
                        P.op("vector",
                             lambda e: e.tensor_mul(
                                 tmp2[0:64, :], eub[0:64, 0:1024],
                                 bcb[0:64, 0:1024]),
                             reads=("eub", "bcb"), writes=("tmp2",))
                        P.op("vector",
                             lambda e: e.tensor_mul(
                                 tmp2[64:128, :], eub[0:64, 1024:2048],
                                 bcb[0:64, 1024:2048]),
                             reads=("eub", "bcb"), writes=("tmp2",))
                        P.op("vector",
                             lambda e, oc=oc: e.scalar_tensor_tensor(
                                 ob[:, oc:oc + QB], tmp2[:, :], lamv[:, 2:3],
                                 ob[:, oc:oc + QB], ALU.mult, ALU.add),
                             reads=("tmp2", "lamv", obkeys[p]), writes=(obkeys[p],))

            stats_rows(ob, obkeys, QB, CCH, 0)
            for ci in range(CCH):
                dma("sync", ldb[:, :], d_xT[ci * 128:(ci + 1) * 128, q0:q0 + QB],
                    ("ldb",), ("ldb",), sL0)
                oc = ci * QB
                P.op("vector",
                     lambda e, oc=oc: e.tensor_mul(
                         tmp2[:, :], ob[:, oc:oc + QB], bcb[:, 0:QB]),
                     reads=(obkeys[ci], "bcb"), writes=("tmp2",))
                P.op("vector",
                     lambda e: e.tensor_sub(tmp2[:, :], tmp2[:, :], bcb[:, QB:2 * QB]),
                     reads=("tmp2", "bcb"), writes=("tmp2",))
                P.op("vector",
                     lambda e, ci=ci: e.tensor_scalar(
                         tmp2[:, :], tmp2[:, :], gp8[:, ci:ci + 1],
                         bp8[:, ci:ci + 1], ALU.mult, ALU.add),
                     reads=("tmp2", "gp8", "bp8"), writes=("tmp2",))
                P.op("vector",
                     lambda e, oc=oc: e.tensor_add(
                         ob[:, oc:oc + QB], tmp2[:, :], ldb[:, :]),
                     reads=("tmp2", "ldb"), writes=(obkeys[ci], "ldb")),
            stats_rows(ob, obkeys, QB, CCH, 0)
            for ci in range(CCH):
                oc = ci * QB
                P.op("vector",
                     lambda e, oc=oc: e.tensor_mul(
                         tmp2[:, :], ob[:, oc:oc + QB], bcb[:, 0:QB]),
                     reads=(obkeys[ci], "bcb"), writes=("tmp2",))
                P.op("vector",
                     lambda e, oc=oc: e.tensor_sub(
                         ob[:, oc:oc + QB], tmp2[:, :], bcb[:, QB:2 * QB]),
                     reads=("tmp2", "bcb"), writes=(obkeys[ci],))
                P.op("vector",
                     lambda e, oc=oc: e.reduce_max(
                         mxt[:, 0:1], ob[:, oc:oc + QB], AX.X,
                         apply_absolute_value=True),
                     reads=(obkeys[ci],), writes=("mxt",))
                P.op("vector",
                     lambda e: e.tensor_max(mx[:, 0:1], mx[:, 0:1], mxt[:, 0:1]),
                     reads=("mxt", "mx"), writes=("mx",))
            dma("gpsimd",
                d_uscr[:, q0:q0 + QB].rearrange("(c p) q -> p c q", p=128),
                ob[:, :].rearrange("p (c q) -> p c q", c=CCH),
                tuple(obkeys), (f"uscr{qb}",), sST)

        P.op("tensor",
             lambda e: e.transpose(pAE[0:1, 0:128], mx[:, 0:1], ident[:, :]),
             reads=("mx", "ident"), writes=("pAE",), sem=sPE)
        P.op("vector", lambda e: e.memset(rows[0:1, 0:4], 0.0), writes=("rows",))
        P.op("vector",
             lambda e: e.reduce_max(rows[0:1, 0:1], pAE[0:1, 0:128], AX.X),
             reads=("pAE",), writes=("rows",))
        dma("gpsimd", d_ccin, rows[0:1, 0:4], ("rows",), ("d_ccin",), sCI)
        P.op("gpsimd",
             lambda e: e.collective_compute(
                 "AllReduce", ALU.max, replica_groups=[list(range(8))],
                 ins=[d_ccin], outs=[d_ccout]),
             reads=("d_ccin",), writes=("d_ccout",), sem=sCC, n=1)
        dma("gpsimd", rows[0:1, 4:8], d_ccout, ("d_ccout",), ("rows",), sCI)
        P.op("vector",
             lambda e: e.tensor_scalar(rows[0:1, 0:1], rows[0:1, 4:5],
                                       1.0 / 127.0, EPS, ALU.mult, ALU.max),
             reads=("rows",), writes=("rows",))
        P.op("vector",
             lambda e: e.reciprocal(rows[0:1, 1:2], rows[0:1, 0:1]),
             reads=("rows",), writes=("rows",))
        bcast(rows[0:1, 0:2], 2, xsin[:, 0:2], "xsin", "rows")
        P.op("vector",
             lambda e: e.tensor_scalar(scl8[:, :], ws8[:, :], xsin[:, 0:1],
                                       None, ALU.mult),
             reads=("ws8", "xsin"), writes=("scl8",))
        dma("sync", wt8v.rearrange("p (c n) -> p c n", c=CCH),
            d_wt8.rearrange("(c p) n -> p c n", p=128), ("kb0", "kb1"),
            ("wt8",), sWT)
        for ci in range(CCH):
            P.op("vector",
                 lambda e, ci=ci: e.tensor_copy(
                     wb[:, ci * C:(ci + 1) * C],
                     wt8v[:, ci * C:(ci + 1) * C]),
                 reads=("wt8",), writes=(f"wb{ci}",))
        for ci in range(CCH):
            for jh in range(2):
                dma("sync", ldb[:, :],
                    d_uscr[ci * 128:(ci + 1) * 128, jh * 1024:(jh + 1) * 1024],
                    ("uscr0", "uscr1", "ldb"), ("ldb",), sL0)
                P.op("vector",
                     lambda e: e.tensor_scalar(
                         bcb[:, 0:1024], ldb[:, :], xsin[:, 1:2], MAGIC,
                         ALU.mult, ALU.add),
                     reads=("ldb", "xsin"), writes=("bcb",))
                P.op("vector",
                     lambda e, ci=ci, jh=jh: e.tensor_scalar(
                         xq[:, ci * T + jh * 1024:ci * T + (jh + 1) * 1024],
                         bcb[:, 0:1024], MAGIC, None, ALU.subtract),
                     reads=("bcb",), writes=(f"xq{ci}", f"xbf{ci}", "ldb"))
        for tb in range(4):
            t0 = tb * 512
            dma("sync", utb.rearrange("p (c q) -> p c q", c=CCH),
                d_uscr[:, t0:t0 + 512].rearrange("(c p) q -> p c q", p=128),
                ("uscr0", "uscr1"),
                ("utb", "kb2", "kb3", "kb4", "kb5"), sUT)
            fkeys = [f"fob{nc_}" for nc_ in range(CCH)]
            for nc_ in range(CCH):
                fo = 0 if (nc_ % 2 == 0) else 1024
                fkey = "st0" if (nc_ % 2 == 0) else "st1"
                for ci in range(CCH):
                    P.op("tensor",
                         lambda e, ci=ci, nc_=nc_, fo=fo, t0=t0: e.matmul(
                             st[:, fo:fo + 512],
                             wb[:, ci * C + nc_ * 128: ci * C + nc_ * 128 + 512 - 384],
                             xq[:, ci * T + t0: ci * T + t0 + 512],
                             start=(ci == 0), stop=(ci == CCH - 1)),
                         reads=(f"wb{ci}", f"xq{ci}"),
                         writes=(fkey,),
                         sem=sPE)
                P.op("vector",
                     lambda e, nc_=nc_, fo=fo: e.tensor_scalar(
                         tmp2[:, 0:512], st[:, fo:fo + 512], scl8[:, nc_:nc_ + 1],
                         fb8[:, nc_:nc_ + 1], ALU.mult, ALU.add),
                     reads=(fkey, "scl8", "fb8"), writes=("tmp2",))
                P.op("vector",
                     lambda e, nc_=nc_: e.tensor_add(
                         fob[:, nc_ * 512:(nc_ + 1) * 512], tmp2[:, 0:512],
                         utb[:, nc_ * 512:(nc_ + 1) * 512]),
                     reads=("tmp2", "utb"),
                     writes=(fkeys[nc_], f"ob{nc_ // 2}"))
            stats_rows(fob, fkeys, 512, CCH, 0)
            for nc_ in range(CCH):
                oc = nc_ * 512
                P.op("vector",
                     lambda e, oc=oc: e.tensor_mul(
                         tmp2[:, 0:512], fob[:, oc:oc + 512], bcb[:, 0:512]),
                     reads=(fkeys[nc_], "bcb"), writes=("tmp2",))
                P.op("vector",
                     lambda e, oc=oc: e.tensor_sub(
                         fob[:, oc:oc + 512], tmp2[:, 0:512], bcb[:, 512:1024]),
                     reads=("tmp2", "bcb"), writes=(fkeys[nc_],))
            dma("gpsimd",
                d_yT[:, t0:t0 + 512].rearrange("(c p) q -> p c q", p=128),
                fob[:, :].rearrange("p (c q) -> p c q", c=CCH),
                tuple(fkeys), (f"y{tb}",), sST)

        v = sST.v
        P.ops["gpsimd"].append(lambda e, v=v: e.wait_ge(sST.h, v))

        with nc.Block() as block:
            @block.sync
            def _(eng):
                for f in P.ops["sync"]:
                    f(eng)

            @block.tensor
            def _(eng):
                for f in P.ops["tensor"]:
                    f(eng)

            @block.scalar
            def _(eng):
                for f in P.ops["scalar"]:
                    f(eng)

            @block.vector
            def _(eng):
                for f in P.ops["vector"]:
                    f(eng)

            @block.gpsimd
            def _(eng):
                for f in P.ops["gpsimd"]:
                    f(eng)

    return nc


_NC_CACHE = {}


def kernel(L0_attn_x, L0_attn_wq1, L0_attn_wk1, L0_attn_wq2, L0_attn_wk2,
           L0_attn_wv, L0_attn_ln_gamma, L0_attn_ln_beta, L0_attn_lamb,
           L0_ffn_weight_int8, L0_ffn_weight_scale, L0_ffn_bias):
    f32 = np.float32
    x = np.asarray(L0_attn_x, f32)
    B = x.shape[0]

    def headmajor(w):
        return np.ascontiguousarray(
            np.asarray(w, f32).transpose(1, 0, 2).reshape(C, C))

    wq1 = headmajor(L0_attn_wq1)
    wk1 = headmajor(L0_attn_wk1)
    wq2 = headmajor(L0_attn_wq2)
    wk2 = headmajor(L0_attn_wk2)
    wv = headmajor(L0_attn_wv)
    wt8 = np.ascontiguousarray(np.asarray(L0_ffn_weight_int8, np.int8).T)
    g8 = np.ascontiguousarray(np.asarray(L0_attn_ln_gamma, f32).reshape(CCH, 128).T)
    b8 = np.ascontiguousarray(np.asarray(L0_attn_ln_beta, f32).reshape(CCH, 128).T)
    ws8 = np.ascontiguousarray(
        np.asarray(L0_ffn_weight_scale, f32).reshape(CCH, 128).T)
    fb8 = np.ascontiguousarray(np.asarray(L0_ffn_bias, f32).reshape(CCH, 128).T)
    lam = np.asarray(L0_attn_lamb, f32).reshape(1, 1)
    ident = np.eye(128, dtype=f32)

    if "nc" not in _NC_CACHE:
        _NC_CACHE["nc"] = build_nc()
    nc = _NC_CACHE["nc"]

    in_maps = []
    for b in range(B):
        in_maps.append(dict(
            xT=np.ascontiguousarray(x[b].T), wq1=wq1, wk1=wk1, wq2=wq2,
            wk2=wk2, wv=wv, wt8=wt8, g8=g8, b8=b8, ws8=ws8, fb8=fb8,
            lam=lam, ident=ident))
    res = run_bass_kernel_spmd(nc, in_maps, core_ids=list(range(B)))
    out = np.stack([np.asarray(res.results[b]["yT"], f32).T for b in range(B)],
                   axis=0)
    return out


# version 05
# speedup vs baseline: 1.0852x; 1.0235x over previous
import contextlib
import numpy as np
import ml_dtypes
import concourse.bass as bass
import concourse.mybir as mybir
from concourse.bass_utils import run_bass_kernel_spmd

F32 = mybir.dt.float32
BF16 = mybir.dt.bfloat16
I8 = mybir.dt.int8
AF = mybir.ActivationFunctionType
ALU = mybir.AluOpType
AX = mybir.AxisListType

C = 1024
T = 2048
H = 16
HS = 64
PAIRS = 8
CCH = 8
SC = 16
QB = 1024
NQB = 2
EPS = 1e-5
MAGIC = float(np.float32(3 * 2.0 ** 22))
VW = 130

ENGINES = ("sync", "tensor", "scalar", "vector", "gpsimd")


class Sem:
    def __init__(self, handle, name, fifo=False, prefix=False):
        self.h = handle
        self.name = name
        self.v = 0
        self.fifo = fifo
        self.waw_ok = False
        self.prefix = prefix


class Plan:

    def __init__(self, sems):
        self.ops = {e: [] for e in ENGINES}
        self.bufs = {}
        self.known = {e: {} for e in ENGINES}
        self.sems = sems

    def _wait(self, eng, tok):
        sem, v = tok
        if getattr(sem, "prefix", False):
            v = sem.v
        if self.known[eng].get(sem.name, 0) >= v:
            return
        self.known[eng][sem.name] = v
        h = sem.h
        self.ops[eng].append(lambda e, h=h, v=v: e.wait_ge(h, v))

    def op(self, eng, build, reads=(), writes=(), sem=None, n=1, skip_self_raw=False):
        if sem is None:
            sem = self.sems[eng]
        for k in reads:
            b = self.bufs.get(k)
            if b and b["w"] is not None:
                ws, wv = b["w"]
                if not (skip_self_raw and ws is sem):
                    self._wait(eng, b["w"])
        for k in writes:
            b = self.bufs.get(k)
            if b:
                if b["w"] is not None:
                    ws, wv = b["w"]
                    if not (ws is sem and sem.waw_ok):
                        self._wait(eng, b["w"])
                for tok in b["r"]:
                    ts_, tv = tok
                    if not (ts_ is sem and sem.waw_ok):
                        self._wait(eng, tok)
        sem.v += n
        v2 = sem.v
        self.ops[eng].append(
            lambda e, b=build, h=sem.h, n=n: b(e).then_inc(h, n))
        tok = (sem, v2)
        for k in writes:
            self.bufs[k] = {"w": tok, "r": []}
        for k in reads:
            b = self.bufs.setdefault(k, {"w": None, "r": []})
            b["r"].append(tok)
        return tok


def build_nc():
    nc = bass.Bass("TRN2", target_bir_lowering=False)

    d_xT = nc.dram_tensor("xT", [C, T], F32, kind="ExternalInput").ap()
    d_wq1 = nc.dram_tensor("wq1", [C, C], F32, kind="ExternalInput").ap()
    d_wk1 = nc.dram_tensor("wk1", [C, C], F32, kind="ExternalInput").ap()
    d_wq2 = nc.dram_tensor("wq2", [C, C], F32, kind="ExternalInput").ap()
    d_wk2 = nc.dram_tensor("wk2", [C, C], F32, kind="ExternalInput").ap()
    d_wv = nc.dram_tensor("wv", [C, C], F32, kind="ExternalInput").ap()
    d_wt8 = nc.dram_tensor("wt8", [C, C], I8, kind="ExternalInput").ap()
    d_g8 = nc.dram_tensor("g8", [128, CCH], F32, kind="ExternalInput").ap()
    d_b8 = nc.dram_tensor("b8", [128, CCH], F32, kind="ExternalInput").ap()
    d_ws8 = nc.dram_tensor("ws8", [128, CCH], F32, kind="ExternalInput").ap()
    d_fb8 = nc.dram_tensor("fb8", [128, CCH], F32, kind="ExternalInput").ap()
    d_lam = nc.dram_tensor("lam", [1, 1], F32, kind="ExternalInput").ap()
    d_ident = nc.dram_tensor("ident", [128, 128], F32, kind="ExternalInput").ap()
    d_yT = nc.dram_tensor("yT", [C, T], F32, kind="ExternalOutput").ap()
    d_uscr = nc.dram_tensor("u_scr", [C, T], F32).ap()
    d_bcs = nc.dram_tensor("bc_scr", [1, 2048], F32).ap()
    d_ccin = nc.dram_tensor("cc_in", [1, 4], F32).ap()
    d_ccout = nc.dram_tensor("cc_out", [1, 4], F32, addr_space="Shared").ap()

    with contextlib.ExitStack() as stk:
        E = stk.enter_context
        xbf = E(nc.sbuf_tensor("xbf", [128, CCH * T], BF16))
        kb = E(nc.sbuf_tensor("kb", [128, PAIRS * T], BF16))
        wb = E(nc.sbuf_tensor("wb", [128, CCH * C], BF16))
        ldb = E(nc.sbuf_tensor("ldb", [128, 1024], F32))
        vaug = E(nc.sbuf_tensor("vaug", [128, PAIRS * SC * VW + 63], BF16))
        qt = E(nc.sbuf_tensor("qt", [128, QB], BF16))
        etb = E(nc.sbuf_tensor("etb", [128, 2 * QB], BF16))
        ob = E(nc.sbuf_tensor("ob", [128, CCH * QB], F32))
        eub = E(nc.sbuf_tensor("eub", [128, 2048], F32))
        bcb = E(nc.sbuf_tensor("bcb", [128, 2048], F32))
        rows = E(nc.sbuf_tensor("rows", [128, 2048], F32))
        tmp2 = E(nc.sbuf_tensor("tmp2", [128, 1024], F32))
        sbb = E(nc.sbuf_tensor("sbb", [128, 1024], BF16))
        sqq = E(nc.sbuf_tensor("sqq", [128, 1024], BF16))
        onesb = E(nc.sbuf_tensor("onesb", [128, 128], BF16))
        epsT = E(nc.sbuf_tensor("epsT", [128, 1], F32))
        ident = E(nc.sbuf_tensor("idents", [128, 128], F32))
        g8 = E(nc.sbuf_tensor("g8s", [128, CCH], F32))
        b8 = E(nc.sbuf_tensor("b8s", [128, CCH], F32))
        ws8 = E(nc.sbuf_tensor("ws8s", [128, CCH], F32))
        fb8 = E(nc.sbuf_tensor("fb8s", [128, CCH], F32))
        gp8 = E(nc.sbuf_tensor("gp8", [128, CCH], F32))
        bp8 = E(nc.sbuf_tensor("bp8", [128, CCH], F32))
        scl8 = E(nc.sbuf_tensor("scl8", [128, CCH], F32))
        lamt = E(nc.sbuf_tensor("lamt", [1, 4], F32))
        lamv = E(nc.sbuf_tensor("lamv", [128, 4], F32))
        xsin = E(nc.sbuf_tensor("xsin", [128, 2], F32))
        mx = E(nc.sbuf_tensor("mx", [128, 1], F32))
        mxt = E(nc.sbuf_tensor("mxt", [128, 1], F32))
        xq = xbf
        wt8v = kb[:, 0:4096].bitcast(I8)
        utb = kb[:, 4096:12288].bitcast(F32)
        fob = ob[:, 0:4096]

        st = E(nc.psum_tensor("st", [128, 2048], F32))
        pAE = E(nc.psum_tensor("pAE", [128, 1024], F32))
        pAO = E(nc.psum_tensor("pAO", [128, 1024], F32))

        sSM = Sem(E(nc.semaphore("sSM")), "sSM", prefix=True)
        sL0 = Sem(E(nc.semaphore("sL0")), "sL0", prefix=True)
        sWT = Sem(E(nc.semaphore("sWT")), "sWT", prefix=True)
        sUT = Sem(E(nc.semaphore("sUT")), "sUT", prefix=True)
        sBC = Sem(E(nc.semaphore("sBC")), "sBC", prefix=True)
        sST = Sem(E(nc.semaphore("sST")), "sST", prefix=True)
        sCI = Sem(E(nc.semaphore("sCI")), "sCI", prefix=True)
        sPE = Sem(E(nc.semaphore("sPE")), "sPE", fifo=True)
        sPE.waw_ok = True
        sACT = Sem(E(nc.semaphore("sACT")), "sACT", fifo=True)
        sDVE = Sem(E(nc.semaphore("sDVE")), "sDVE", fifo=True)
        sCC = Sem(E(nc.semaphore("sCC")), "sCC")

        P = Plan({"sync": sSM, "tensor": sPE, "scalar": sACT,
                  "vector": sDVE, "gpsimd": sST})

        def slot(base, col, n=1024):
            return rows[base:base + 1, col * n:col * n + n]

        def dma(eng, out, in_, reads, writes, sem):
            P.op(eng, lambda e, o=out, i=in_: e.dma_start(out=o, in_=i),
                 reads=reads, writes=writes, sem=sem, n=16)

        def bcast(src_row, width, dst, dst_key, src_key):
            dma("gpsimd", d_bcs[0:1, 0:width], src_row, (src_key,), ("d_bcs",), sBC)
            dma("gpsimd", dst, d_bcs[0:1, 0:width].partition_broadcast(128),
                ("d_bcs",), (dst_key,), sBC)

        dma("sync", g8[:, :], d_g8, (), ("g8",), sSM)
        dma("sync", b8[:, :], d_b8, (), ("b8",), sSM)
        dma("sync", ws8[:, :], d_ws8, (), ("ws8",), sSM)
        dma("sync", fb8[:, :], d_fb8, (), ("fb8",), sSM)
        dma("sync", lamt[0:1, 0:1], d_lam, (), ("lamt",), sSM)
        dma("sync", ident[:, :], d_ident, (), ("ident",), sSM)
        P.op("vector", lambda e: e.memset(vaug[:, :], 1.0), writes=("vaug",))
        P.op("vector", lambda e: e.memset(onesb[:, :], 1.0), writes=("onesb",))
        P.op("vector", lambda e: e.memset(epsT[:, :], EPS), writes=("epsT",))
        P.op("vector", lambda e: e.memset(mx[:, :], 0.0), writes=("mx",))
        P.op("vector",
             lambda e: e.tensor_scalar(lamt[0:1, 1:2], lamt[0:1, 0:1], -1.0, 1.0,
                                       ALU.mult, ALU.add),
             reads=("lamt",), writes=("lamt",))
        P.op("vector",
             lambda e: e.tensor_scalar(lamt[0:1, 2:3], lamt[0:1, 0:1], -1.0, None,
                                       ALU.mult),
             reads=("lamt",), writes=("lamt",))
        P.op("vector", lambda e: e.memset(lamt[0:1, 3:4], 0.0),
             reads=("lamt",), writes=("lamt",))
        bcast(lamt[0:1, 0:4], 4, lamv[:, 0:4], "lamv", "lamt")
        P.op("vector",
             lambda e: e.tensor_scalar(gp8[:, :], g8[:, :], lamv[:, 1:2], None, ALU.mult),
             reads=("g8", "lamv"), writes=("gp8",))
        P.op("vector",
             lambda e: e.tensor_scalar(bp8[:, :], b8[:, :], lamv[:, 1:2], None, ALU.mult),
             reads=("b8", "lamv"), writes=("bp8",))

        for ci in range(CCH):
            for jh in range(2):
                dma("sync", ldb[:, :],
                    d_xT[ci * 128:(ci + 1) * 128, jh * 1024:(jh + 1) * 1024],
                    ("ldb",), ("ldb",), sL0)
                P.op("vector",
                     lambda e, ci=ci, jh=jh: e.tensor_copy(
                         xbf[:, ci * T + jh * 1024: ci * T + (jh + 1) * 1024],
                         ldb[:, :]),
                     reads=("ldb",), writes=(f"xbf{ci}",))

        def load_weight(dram):
            for ci in range(CCH):
                dma("sync", ldb[:, :], dram[ci * 128:(ci + 1) * 128, :],
                    ("ldb",), ("ldb",), sL0)
                P.op("vector",
                     lambda e, ci=ci: e.tensor_copy(
                         wb[:, ci * C:(ci + 1) * C], ldb[:, :]),
                     reads=("ldb",), writes=(f"wb{ci}",))

        load_weight(d_wv)
        for p in range(PAIRS):
            for ci in range(CCH):
                for j in range(4):
                    P.op("tensor",
                         lambda e, ci=ci, j=j, p=p: e.matmul(
                             st[:, j * 512:(j + 1) * 512],
                             wb[:, ci * C + p * 128: ci * C + p * 128 + 128],
                             xbf[:, ci * T + j * 512: ci * T + (j + 1) * 512],
                             start=(ci == 0), stop=(ci == CCH - 1)),
                         reads=(f"wb{ci}", f"xbf{ci}"),
                         writes=("st0", "st1"),
                         sem=sPE)
            P.op("vector", lambda e: e.tensor_copy(eub[:, :], st[:, :]),
                 reads=("st0", "st1"), writes=("eub",))
            for k in range(SC):
                pA, pkey = (pAE, "pAE") if (k % 2 == 0) else (pAO, "pAO")
                P.op("tensor",
                     lambda e, k=k, pA=pA: e.transpose(
                         pA[:, 0:128], eub[:, k * 128:(k + 1) * 128], ident[:, :]),
                     reads=("eub", "ident"), writes=(pkey,), sem=sPE)
                vo = (p * SC + k) * VW
                P.op("vector",
                     lambda e, vo=vo, pA=pA: e.tensor_copy(
                         vaug[:, vo:vo + 64], pA[:, 0:64]),
                     reads=(pkey,), writes=("vaug",))
                P.op("vector",
                     lambda e, vo=vo, pA=pA: e.tensor_copy(
                         vaug[:, vo + 65:vo + 129], pA[:, 64:128]),
                     reads=(pkey,), writes=("vaug",))

        def stats_rows(src, src_keys, n, ncch, col0):
            nj = (n + 511) // 512
            for ci in range(ncch):
                P.op("vector",
                     lambda e, ci=ci: e.tensor_copy(
                         sbb[:, 0:n], src[:, ci * n + col0: ci * n + col0 + n]),
                     reads=(src_keys[ci],), writes=("sbb",))
                P.op("vector",
                     lambda e: e.tensor_mul(sqq[:, 0:n], sbb[:, 0:n], sbb[:, 0:n]),
                     reads=("sbb",), writes=("sqq",))
                for j in range(nj):
                    w = min(512, n - j * 512)
                    P.op("tensor",
                         lambda e, j=j, w=w, ci=ci: e.matmul(
                             pAE[:, j * 512:j * 512 + w], onesb[:, :],
                             sbb[:, j * 512:j * 512 + w],
                             start=(ci == 0), stop=(ci == ncch - 1)),
                         reads=("onesb", "sbb"),
                         writes=("pAE",),
                         sem=sPE)
                    P.op("tensor",
                         lambda e, j=j, w=w, ci=ci: e.matmul(
                             pAO[:, j * 512:j * 512 + w], onesb[:, :],
                             sqq[:, j * 512:j * 512 + w],
                             start=(ci == 0), stop=(ci == ncch - 1)),
                         reads=("onesb", "sqq"),
                         writes=("pAO",),
                         sem=sPE)
            P.op("vector", lambda e: e.tensor_scalar(
                slot(0, 0, n), pAE[0:1, 0:n], 1.0 / C, None, ALU.mult),
                reads=("pAE",), writes=("rows",))
            P.op("vector", lambda e: e.tensor_scalar(
                slot(0, 1, n), pAO[0:1, 0:n], 1.0 / C, None, ALU.mult),
                reads=("pAO",), writes=("rows",))
            P.op("vector", lambda e: e.tensor_mul(
                slot(32, 0, n), slot(0, 0, n), slot(0, 0, n)),
                reads=("rows",), writes=("rows",))
            P.op("vector", lambda e: e.tensor_copy(
                slot(32, 1, n), slot(0, 1, n)),
                reads=("rows",), writes=("rows",))
            P.op("vector", lambda e: e.tensor_sub(
                slot(32, 1, n), slot(32, 1, n), slot(32, 0, n)),
                reads=("rows",), writes=("rows",))
            P.op("scalar", lambda e: e.activation(
                slot(32, 0, n), slot(32, 1, n), AF.Ln,
                bias=epsT[0:1, 0:1], scale=1.0),
                reads=("rows", "epsT"), writes=("rows",))
            P.op("scalar", lambda e: e.activation(
                slot(64, 0, n), slot(32, 0, n), AF.Exp, bias=0.0, scale=-0.5),
                reads=("rows",), writes=("rows",))
            P.op("vector", lambda e: e.tensor_copy(
                slot(64, 1, n), slot(0, 0, n)),
                reads=("rows",), writes=("rows",))
            P.op("vector", lambda e: e.tensor_mul(
                slot(64, 1, n), slot(64, 0, n), slot(64, 1, n)),
                reads=("rows",), writes=("rows",))
            bcast(rows[64:65, 0:2 * n], 2 * n, bcb[:, 0:2 * n], "bcb", "rows")

        for qb in range(NQB):
            q0 = qb * QB
            obkeys = [f"ob{ci}" for ci in range(CCH)]
            for br in range(2):
                load_weight(d_wk1 if br == 0 else d_wk2)
                for p in range(PAIRS):
                    for ci in range(CCH):
                        for j in range(4):
                            P.op("tensor",
                                 lambda e, j=j, ci=ci, p=p: e.matmul(
                                     st[:, j * 512:(j + 1) * 512],
                                     wb[:, ci * C + p * 128: ci * C + p * 128 + 128],
                                     xbf[:, ci * T + j * 512: ci * T + (j + 1) * 512],
                                     start=(ci == 0), stop=(ci == CCH - 1)),
                                 reads=(f"wb{ci}", f"xbf{ci}"),
                                 writes=("st0", "st1"),
                                 sem=sPE)
                    P.op("vector",
                         lambda e, p=p: e.tensor_copy(
                             kb[:, p * T:(p + 1) * T], st[:, :]),
                         reads=("st0", "st1"), writes=(f"kb{p}",))
                load_weight(d_wq1 if br == 0 else d_wq2)
                for p in range(PAIRS):
                    for ci in range(CCH):
                        for j in range(2):
                            P.op("tensor",
                                 lambda e, j=j, ci=ci, p=p, q0=q0: e.matmul(
                                     st[:, j * 512:(j + 1) * 512],
                                     wb[:, ci * C + p * 128: ci * C + p * 128 + 128],
                                     xbf[:, ci * T + q0 + j * 512:
                                         ci * T + q0 + (j + 1) * 512],
                                     start=(ci == 0), stop=(ci == CCH - 1)),
                                 reads=(f"wb{ci}", f"xbf{ci}"),
                                 writes=("st0",),
                                 sem=sPE)
                    P.op("vector",
                         lambda e: e.tensor_copy(qt[:, :], st[:, 0:QB]),
                         reads=("st0",), writes=("qt",))
                    def emit_scores(h, sc):
                        ko = p * T + sc * 128
                        c0 = h * 1024
                        pb = 0 if h == 0 else 64
                        skey = "st0" if h == 0 else "st1"
                        for j in range(2):
                            P.op("tensor",
                                 lambda e, j=j, ko=ko, c0=c0, pb=pb: e.matmul(
                                     st[:, c0 + j * 512:c0 + (j + 1) * 512],
                                     kb[pb:pb + 64, ko:ko + 128],
                                     qt[pb:pb + 64, j * 512:(j + 1) * 512],
                                     start=True, stop=True,
                                     tile_position=(pb, 0)),
                                 reads=(f"kb{p}", "qt"),
                                 writes=(skey,),
                                 sem=sPE)

                    def emit_av(h, sc):
                        vo = (p * SC + sc) * VW + (0 if h == 0 else 65)
                        c0 = h * 1024
                        pA, pkey = (pAE, "pAE") if h == 0 else (pAO, "pAO")
                        ekey = "etE" if h == 0 else "etO"
                        for j in range(2):
                            P.op("tensor",
                                 lambda e, j=j, vo=vo, c0=c0, pA=pA, sc=sc: e.matmul(
                                     pA[:, j * 512:(j + 1) * 512],
                                     vaug[:, vo:vo + 128],
                                     etb[:, c0 + j * 512:c0 + (j + 1) * 512],
                                     start=(sc == 0), stop=(sc == SC - 1)),
                                 reads=("vaug", ekey),
                                 writes=(pkey,),
                                 sem=sPE)

                    def emit_exp(h, sc):
                        c0 = h * 1024
                        skey = "st0" if h == 0 else "st1"
                        ekey = "etE" if h == 0 else "etO"
                        P.op("scalar",
                             lambda e, c0=c0: e.activation(
                                 etb[:, c0:c0 + 1024], st[:, c0:c0 + 1024],
                                 AF.Exp, bias=0.0, scale=0.125),
                             reads=(skey,), writes=(ekey,))

                    prev = None
                    for sc in range(SC):
                        for h in (0, 1):
                            emit_scores(h, sc)
                            if prev is not None:
                                emit_av(*prev)
                            emit_exp(h, sc)
                            prev = (h, sc)
                    emit_av(*prev)
                    P.op("vector",
                         lambda e: e.tensor_copy(eub[0:65, 0:1024], pAE[0:65, :]),
                         reads=("pAE",), writes=("eub",))
                    P.op("vector",
                         lambda e: e.tensor_copy(eub[0:65, 1024:2048], pAO[0:65, :]),
                         reads=("pAO",), writes=("eub",))
                    P.op("vector",
                         lambda e: e.reciprocal(rows[64:65, 0:2048],
                                                eub[64:65, 0:2048]),
                         reads=("eub",), writes=("rows",))
                    bcast(rows[64:65, 0:2048], 2048, bcb[:, 0:2048], "bcb", "rows")
                    oc = p * QB
                    if br == 0:
                        P.op("vector",
                             lambda e, oc=oc: e.tensor_mul(
                                 ob[0:64, oc:oc + QB], eub[0:64, 0:1024],
                                 bcb[0:64, 0:1024]),
                             reads=("eub", "bcb"), writes=(obkeys[p],))
                        P.op("vector",
                             lambda e, oc=oc: e.tensor_mul(
                                 ob[64:128, oc:oc + QB], eub[0:64, 1024:2048],
                                 bcb[0:64, 1024:2048]),
                             reads=("eub", "bcb"), writes=(obkeys[p],))
                    else:
                        P.op("vector",
                             lambda e: e.tensor_mul(
                                 tmp2[0:64, :], eub[0:64, 0:1024],
                                 bcb[0:64, 0:1024]),
                             reads=("eub", "bcb"), writes=("tmp2",))
                        P.op("vector",
                             lambda e: e.tensor_mul(
                                 tmp2[64:128, :], eub[0:64, 1024:2048],
                                 bcb[0:64, 1024:2048]),
                             reads=("eub", "bcb"), writes=("tmp2",))
                        P.op("vector",
                             lambda e, oc=oc: e.scalar_tensor_tensor(
                                 ob[:, oc:oc + QB], tmp2[:, :], lamv[:, 2:3],
                                 ob[:, oc:oc + QB], ALU.mult, ALU.add),
                             reads=("tmp2", "lamv", obkeys[p]), writes=(obkeys[p],))

            stats_rows(ob, obkeys, QB, CCH, 0)
            for ci in range(CCH):
                dma("sync", ldb[:, :], d_xT[ci * 128:(ci + 1) * 128, q0:q0 + QB],
                    ("ldb",), ("ldb",), sL0)
                oc = ci * QB
                P.op("vector",
                     lambda e, oc=oc: e.tensor_mul(
                         tmp2[:, :], ob[:, oc:oc + QB], bcb[:, 0:QB]),
                     reads=(obkeys[ci], "bcb"), writes=("tmp2",))
                P.op("vector",
                     lambda e: e.tensor_sub(tmp2[:, :], tmp2[:, :], bcb[:, QB:2 * QB]),
                     reads=("tmp2", "bcb"), writes=("tmp2",))
                P.op("vector",
                     lambda e, ci=ci: e.tensor_scalar(
                         tmp2[:, :], tmp2[:, :], gp8[:, ci:ci + 1],
                         bp8[:, ci:ci + 1], ALU.mult, ALU.add),
                     reads=("tmp2", "gp8", "bp8"), writes=("tmp2",))
                P.op("vector",
                     lambda e, oc=oc: e.tensor_add(
                         ob[:, oc:oc + QB], tmp2[:, :], ldb[:, :]),
                     reads=("tmp2", "ldb"), writes=(obkeys[ci], "ldb")),
            stats_rows(ob, obkeys, QB, CCH, 0)
            for ci in range(CCH):
                oc = ci * QB
                P.op("vector",
                     lambda e, oc=oc: e.tensor_mul(
                         tmp2[:, :], ob[:, oc:oc + QB], bcb[:, 0:QB]),
                     reads=(obkeys[ci], "bcb"), writes=("tmp2",))
                P.op("vector",
                     lambda e, oc=oc: e.tensor_sub(
                         ob[:, oc:oc + QB], tmp2[:, :], bcb[:, QB:2 * QB]),
                     reads=("tmp2", "bcb"), writes=(obkeys[ci],))
                P.op("vector",
                     lambda e, oc=oc: e.reduce_max(
                         mxt[:, 0:1], ob[:, oc:oc + QB], AX.X,
                         apply_absolute_value=True),
                     reads=(obkeys[ci],), writes=("mxt",))
                P.op("vector",
                     lambda e: e.tensor_max(mx[:, 0:1], mx[:, 0:1], mxt[:, 0:1]),
                     reads=("mxt", "mx"), writes=("mx",))
            dma("gpsimd",
                d_uscr[:, q0:q0 + QB].rearrange("(c p) q -> p c q", p=128),
                ob[:, :].rearrange("p (c q) -> p c q", c=CCH),
                tuple(obkeys), (f"uscr{qb}",), sST)

        P.op("tensor",
             lambda e: e.transpose(pAE[0:1, 0:128], mx[:, 0:1], ident[:, :]),
             reads=("mx", "ident"), writes=("pAE",), sem=sPE)
        P.op("vector", lambda e: e.memset(rows[0:1, 0:4], 0.0), writes=("rows",))
        P.op("vector",
             lambda e: e.reduce_max(rows[0:1, 0:1], pAE[0:1, 0:128], AX.X),
             reads=("pAE",), writes=("rows",))
        dma("gpsimd", d_ccin, rows[0:1, 0:4], ("rows",), ("d_ccin",), sCI)
        P.op("gpsimd",
             lambda e: e.collective_compute(
                 "AllReduce", ALU.max, replica_groups=[list(range(8))],
                 ins=[d_ccin], outs=[d_ccout]),
             reads=("d_ccin",), writes=("d_ccout",), sem=sCC, n=1)
        dma("gpsimd", rows[0:1, 4:8], d_ccout, ("d_ccout",), ("rows",), sCI)
        P.op("vector",
             lambda e: e.tensor_scalar(rows[0:1, 0:1], rows[0:1, 4:5],
                                       1.0 / 127.0, EPS, ALU.mult, ALU.max),
             reads=("rows",), writes=("rows",))
        P.op("vector",
             lambda e: e.reciprocal(rows[0:1, 1:2], rows[0:1, 0:1]),
             reads=("rows",), writes=("rows",))
        bcast(rows[0:1, 0:2], 2, xsin[:, 0:2], "xsin", "rows")
        P.op("vector",
             lambda e: e.tensor_scalar(scl8[:, :], ws8[:, :], xsin[:, 0:1],
                                       None, ALU.mult),
             reads=("ws8", "xsin"), writes=("scl8",))
        dma("sync", wt8v.rearrange("p (c n) -> p c n", c=CCH),
            d_wt8.rearrange("(c p) n -> p c n", p=128), ("kb0", "kb1"),
            ("wt8",), sWT)
        for ci in range(CCH):
            P.op("vector",
                 lambda e, ci=ci: e.tensor_copy(
                     wb[:, ci * C:(ci + 1) * C],
                     wt8v[:, ci * C:(ci + 1) * C]),
                 reads=("wt8",), writes=(f"wb{ci}",))
        for ci in range(CCH):
            for jh in range(2):
                dma("sync", ldb[:, :],
                    d_uscr[ci * 128:(ci + 1) * 128, jh * 1024:(jh + 1) * 1024],
                    ("uscr0", "uscr1", "ldb"), ("ldb",), sL0)
                P.op("vector",
                     lambda e: e.tensor_scalar(
                         bcb[:, 0:1024], ldb[:, :], xsin[:, 1:2], MAGIC,
                         ALU.mult, ALU.add),
                     reads=("ldb", "xsin"), writes=("bcb",))
                P.op("vector",
                     lambda e, ci=ci, jh=jh: e.tensor_scalar(
                         xq[:, ci * T + jh * 1024:ci * T + (jh + 1) * 1024],
                         bcb[:, 0:1024], MAGIC, None, ALU.subtract),
                     reads=("bcb",), writes=(f"xq{ci}", f"xbf{ci}", "ldb"))
        for tb in range(4):
            t0 = tb * 512
            dma("sync", utb.rearrange("p (c q) -> p c q", c=CCH),
                d_uscr[:, t0:t0 + 512].rearrange("(c p) q -> p c q", p=128),
                ("uscr0", "uscr1"),
                ("utb", "kb2", "kb3", "kb4", "kb5"), sUT)
            fkeys = [f"fob{nc_}" for nc_ in range(CCH)]
            for nc_ in range(CCH):
                fo = 0 if (nc_ % 2 == 0) else 1024
                fkey = "st0" if (nc_ % 2 == 0) else "st1"
                for ci in range(CCH):
                    P.op("tensor",
                         lambda e, ci=ci, nc_=nc_, fo=fo, t0=t0: e.matmul(
                             st[:, fo:fo + 512],
                             wb[:, ci * C + nc_ * 128: ci * C + nc_ * 128 + 512 - 384],
                             xq[:, ci * T + t0: ci * T + t0 + 512],
                             start=(ci == 0), stop=(ci == CCH - 1)),
                         reads=(f"wb{ci}", f"xq{ci}"),
                         writes=(fkey,),
                         sem=sPE)
                P.op("vector",
                     lambda e, nc_=nc_, fo=fo: e.tensor_scalar(
                         tmp2[:, 0:512], st[:, fo:fo + 512], scl8[:, nc_:nc_ + 1],
                         fb8[:, nc_:nc_ + 1], ALU.mult, ALU.add),
                     reads=(fkey, "scl8", "fb8"), writes=("tmp2",))
                P.op("vector",
                     lambda e, nc_=nc_: e.tensor_add(
                         fob[:, nc_ * 512:(nc_ + 1) * 512], tmp2[:, 0:512],
                         utb[:, nc_ * 512:(nc_ + 1) * 512]),
                     reads=("tmp2", "utb"),
                     writes=(fkeys[nc_], f"ob{nc_ // 2}"))
            stats_rows(fob, fkeys, 512, CCH, 0)
            for nc_ in range(CCH):
                oc = nc_ * 512
                P.op("vector",
                     lambda e, oc=oc: e.tensor_mul(
                         tmp2[:, 0:512], fob[:, oc:oc + 512], bcb[:, 0:512]),
                     reads=(fkeys[nc_], "bcb"), writes=("tmp2",))
                P.op("vector",
                     lambda e, oc=oc: e.tensor_sub(
                         fob[:, oc:oc + 512], tmp2[:, 0:512], bcb[:, 512:1024]),
                     reads=("tmp2", "bcb"), writes=(fkeys[nc_],))
            dma("gpsimd",
                d_yT[:, t0:t0 + 512].rearrange("(c p) q -> p c q", p=128),
                fob[:, :].rearrange("p (c q) -> p c q", c=CCH),
                tuple(fkeys), (f"y{tb}",), sST)

        v = sST.v
        P.ops["gpsimd"].append(lambda e, v=v: e.wait_ge(sST.h, v))

        with nc.Block() as block:
            @block.sync
            def _(eng):
                for f in P.ops["sync"]:
                    f(eng)

            @block.tensor
            def _(eng):
                for f in P.ops["tensor"]:
                    f(eng)

            @block.scalar
            def _(eng):
                for f in P.ops["scalar"]:
                    f(eng)

            @block.vector
            def _(eng):
                for f in P.ops["vector"]:
                    f(eng)

            @block.gpsimd
            def _(eng):
                for f in P.ops["gpsimd"]:
                    f(eng)

    return nc


_NC_CACHE = {}


def kernel(L0_attn_x, L0_attn_wq1, L0_attn_wk1, L0_attn_wq2, L0_attn_wk2,
           L0_attn_wv, L0_attn_ln_gamma, L0_attn_ln_beta, L0_attn_lamb,
           L0_ffn_weight_int8, L0_ffn_weight_scale, L0_ffn_bias):
    f32 = np.float32
    x = np.asarray(L0_attn_x, f32)
    B = x.shape[0]

    def headmajor(w):
        return np.ascontiguousarray(
            np.asarray(w, f32).transpose(1, 0, 2).reshape(C, C))

    wq1 = headmajor(L0_attn_wq1)
    wk1 = headmajor(L0_attn_wk1)
    wq2 = headmajor(L0_attn_wq2)
    wk2 = headmajor(L0_attn_wk2)
    wv = headmajor(L0_attn_wv)
    wt8 = np.ascontiguousarray(np.asarray(L0_ffn_weight_int8, np.int8).T)
    g8 = np.ascontiguousarray(np.asarray(L0_attn_ln_gamma, f32).reshape(CCH, 128).T)
    b8 = np.ascontiguousarray(np.asarray(L0_attn_ln_beta, f32).reshape(CCH, 128).T)
    ws8 = np.ascontiguousarray(
        np.asarray(L0_ffn_weight_scale, f32).reshape(CCH, 128).T)
    fb8 = np.ascontiguousarray(np.asarray(L0_ffn_bias, f32).reshape(CCH, 128).T)
    lam = np.asarray(L0_attn_lamb, f32).reshape(1, 1)
    ident = np.eye(128, dtype=f32)

    if "nc" not in _NC_CACHE:
        _NC_CACHE["nc"] = build_nc()
    nc = _NC_CACHE["nc"]

    in_maps = []
    for b in range(B):
        in_maps.append(dict(
            xT=np.ascontiguousarray(x[b].T), wq1=wq1, wk1=wk1, wq2=wq2,
            wk2=wk2, wv=wv, wt8=wt8, g8=g8, b8=b8, ws8=ws8, fb8=fb8,
            lam=lam, ident=ident))
    res = run_bass_kernel_spmd(nc, in_maps, core_ids=list(range(B)))
    out = np.stack([np.asarray(res.results[b]["yT"], f32).T for b in range(B)],
                   axis=0)
    return out


# version 06 (final)
# speedup vs baseline: 1.2961x; 1.1944x over previous
import contextlib
import numpy as np
import ml_dtypes
import concourse.bass as bass
import concourse.mybir as mybir
from concourse.bass_utils import run_bass_kernel_spmd

F32 = mybir.dt.float32
BF16 = mybir.dt.bfloat16
I8 = mybir.dt.int8
AF = mybir.ActivationFunctionType
ALU = mybir.AluOpType
AX = mybir.AxisListType

C = 1024
T = 2048
H = 16
HS = 64
PAIRS = 8
CCH = 8
SC = 16
QB = 1024
NQB = 2
EPS = 1e-5
MAGIC = float(np.float32(3 * 2.0 ** 22))
VW = 130

ENGINES = ("sync", "tensor", "scalar", "vector", "gpsimd")


class Sem:
    def __init__(self, handle, name, fifo=False, prefix=False):
        self.h = handle
        self.name = name
        self.v = 0
        self.fifo = fifo
        self.waw_ok = False
        self.prefix = prefix


class Plan:

    def __init__(self, sems):
        self.ops = {e: [] for e in ENGINES}
        self.bufs = {}
        self.known = {e: {} for e in ENGINES}
        self.sems = sems

    def _wait(self, eng, tok):
        sem, v = tok
        if getattr(sem, "prefix", False):
            v = sem.v
        if self.known[eng].get(sem.name, 0) >= v:
            return
        self.known[eng][sem.name] = v
        h = sem.h
        self.ops[eng].append(lambda e, h=h, v=v: e.wait_ge(h, v))

    def op(self, eng, build, reads=(), writes=(), sem=None, n=1, skip_self_raw=False):
        if sem is None:
            sem = self.sems[eng]
        for k in reads:
            b = self.bufs.get(k)
            if b and b["w"] is not None:
                ws, wv = b["w"]
                if not (skip_self_raw and ws is sem):
                    self._wait(eng, b["w"])
        for k in writes:
            b = self.bufs.get(k)
            if b:
                if b["w"] is not None:
                    ws, wv = b["w"]
                    if not (ws is sem and sem.waw_ok):
                        self._wait(eng, b["w"])
                for tok in b["r"]:
                    ts_, tv = tok
                    if not (ts_ is sem and sem.waw_ok):
                        self._wait(eng, tok)
        sem.v += n
        v2 = sem.v
        self.ops[eng].append(
            lambda e, b=build, h=sem.h, n=n: b(e).then_inc(h, n))
        tok = (sem, v2)
        for k in writes:
            self.bufs[k] = {"w": tok, "r": []}
        for k in reads:
            b = self.bufs.setdefault(k, {"w": None, "r": []})
            b["r"].append(tok)
        return tok


def build_nc():
    nc = bass.Bass("TRN2", target_bir_lowering=False)

    d_xT = nc.dram_tensor("xT", [C, T], F32, kind="ExternalInput").ap()
    d_wq1 = nc.dram_tensor("wq1", [C, C], F32, kind="ExternalInput").ap()
    d_wk1 = nc.dram_tensor("wk1", [C, C], F32, kind="ExternalInput").ap()
    d_wq2 = nc.dram_tensor("wq2", [C, C], F32, kind="ExternalInput").ap()
    d_wk2 = nc.dram_tensor("wk2", [C, C], F32, kind="ExternalInput").ap()
    d_wv = nc.dram_tensor("wv", [C, C], F32, kind="ExternalInput").ap()
    d_wt8 = nc.dram_tensor("wt8", [C, C], I8, kind="ExternalInput").ap()
    d_g8 = nc.dram_tensor("g8", [128, CCH], F32, kind="ExternalInput").ap()
    d_b8 = nc.dram_tensor("b8", [128, CCH], F32, kind="ExternalInput").ap()
    d_ws8 = nc.dram_tensor("ws8", [128, CCH], F32, kind="ExternalInput").ap()
    d_fb8 = nc.dram_tensor("fb8", [128, CCH], F32, kind="ExternalInput").ap()
    d_lam = nc.dram_tensor("lam", [1, 1], F32, kind="ExternalInput").ap()
    d_ident = nc.dram_tensor("ident", [128, 128], F32, kind="ExternalInput").ap()
    d_yT = nc.dram_tensor("yT", [C, T], F32, kind="ExternalOutput").ap()
    d_uscr = nc.dram_tensor("u_scr", [C, T], F32).ap()
    d_bcs = nc.dram_tensor("bc_scr", [1, 2048], F32).ap()
    d_ccin = nc.dram_tensor("cc_in", [1, 4], F32).ap()
    d_ccout = nc.dram_tensor("cc_out", [1, 4], F32, addr_space="Shared").ap()

    with contextlib.ExitStack() as stk:
        E = stk.enter_context
        xbf = E(nc.sbuf_tensor("xbf", [128, CCH * T], BF16))
        kb = E(nc.sbuf_tensor("kb", [128, PAIRS * T], BF16))
        wb = E(nc.sbuf_tensor("wb", [128, CCH * C], BF16))
        ldb = E(nc.sbuf_tensor("ldb", [128, 1024], F32))
        vaug = E(nc.sbuf_tensor("vaug", [128, PAIRS * SC * VW + 63], BF16))
        qt = E(nc.sbuf_tensor("qt", [128, QB], BF16))
        etb = E(nc.sbuf_tensor("etb", [128, 2 * QB], BF16))
        ob = E(nc.sbuf_tensor("ob", [128, CCH * QB], F32))
        eub = E(nc.sbuf_tensor("eub", [128, 2048], F32))
        bcb = E(nc.sbuf_tensor("bcb", [128, 2048], F32))
        rows = E(nc.sbuf_tensor("rows", [128, 2048], F32))
        tmp2 = E(nc.sbuf_tensor("tmp2", [128, 1024], F32))
        sbb = E(nc.sbuf_tensor("sbb", [128, 1024], BF16))
        sqq = E(nc.sbuf_tensor("sqq", [128, 1024], BF16))
        onesb = E(nc.sbuf_tensor("onesb", [128, 128], BF16))
        epsT = E(nc.sbuf_tensor("epsT", [128, 1], F32))
        ident = E(nc.sbuf_tensor("idents", [128, 128], F32))
        g8 = E(nc.sbuf_tensor("g8s", [128, CCH], F32))
        b8 = E(nc.sbuf_tensor("b8s", [128, CCH], F32))
        ws8 = E(nc.sbuf_tensor("ws8s", [128, CCH], F32))
        fb8 = E(nc.sbuf_tensor("fb8s", [128, CCH], F32))
        gp8 = E(nc.sbuf_tensor("gp8", [128, CCH], F32))
        bp8 = E(nc.sbuf_tensor("bp8", [128, CCH], F32))
        scl8 = E(nc.sbuf_tensor("scl8", [128, CCH], F32))
        lamt = E(nc.sbuf_tensor("lamt", [1, 4], F32))
        lamv = E(nc.sbuf_tensor("lamv", [128, 4], F32))
        xsin = E(nc.sbuf_tensor("xsin", [128, 2], F32))
        mx = E(nc.sbuf_tensor("mx", [128, 1], F32))
        mxt = E(nc.sbuf_tensor("mxt", [128, 1], F32))
        xq = xbf
        wt8v = kb[:, 0:4096].bitcast(I8)
        utb = kb[:, 4096:12288].bitcast(F32)
        fob = ob[:, 0:4096]

        st = E(nc.psum_tensor("st", [128, 2048], F32))
        pAE = E(nc.psum_tensor("pAE", [128, 1024], F32))
        pAO = E(nc.psum_tensor("pAO", [128, 1024], F32))

        sSM = Sem(E(nc.semaphore("sSM")), "sSM", prefix=True)
        sL0 = Sem(E(nc.semaphore("sL0")), "sL0", prefix=True)
        sWT = Sem(E(nc.semaphore("sWT")), "sWT", prefix=True)
        sUT = Sem(E(nc.semaphore("sUT")), "sUT", prefix=True)
        sBC = Sem(E(nc.semaphore("sBC")), "sBC", prefix=True)
        sST = Sem(E(nc.semaphore("sST")), "sST", prefix=True)
        sCI = Sem(E(nc.semaphore("sCI")), "sCI", prefix=True)
        sPE = Sem(E(nc.semaphore("sPE")), "sPE", fifo=True)
        sPE.waw_ok = True
        sACT = Sem(E(nc.semaphore("sACT")), "sACT", fifo=True)
        sDVE = Sem(E(nc.semaphore("sDVE")), "sDVE", fifo=True)
        sCC = Sem(E(nc.semaphore("sCC")), "sCC")

        P = Plan({"sync": sSM, "tensor": sPE, "scalar": sACT,
                  "vector": sDVE, "gpsimd": sST})

        def slot(base, col, n=1024):
            return rows[base:base + 1, col * n:col * n + n]

        def dma(eng, out, in_, reads, writes, sem):
            P.op(eng, lambda e, o=out, i=in_: e.dma_start(out=o, in_=i),
                 reads=reads, writes=writes, sem=sem, n=16)

        def bcast(src_row, width, dst, dst_key, src_key):
            dma("gpsimd", d_bcs[0:1, 0:width], src_row, (src_key,), ("d_bcs",), sBC)
            dma("gpsimd", dst, d_bcs[0:1, 0:width].partition_broadcast(128),
                ("d_bcs",), (dst_key,), sBC)

        dma("sync", g8[:, :], d_g8, (), ("g8",), sSM)
        dma("sync", b8[:, :], d_b8, (), ("b8",), sSM)
        dma("sync", ws8[:, :], d_ws8, (), ("ws8",), sSM)
        dma("sync", fb8[:, :], d_fb8, (), ("fb8",), sSM)
        dma("sync", lamt[0:1, 0:1], d_lam, (), ("lamt",), sSM)
        dma("sync", ident[:, :], d_ident, (), ("ident",), sSM)
        P.op("vector", lambda e: e.memset(vaug[:, :], 1.0), writes=("vaug",))
        P.op("vector", lambda e: e.memset(onesb[:, :], 1.0), writes=("onesb",))
        P.op("vector", lambda e: e.memset(epsT[:, :], EPS), writes=("epsT",))
        P.op("vector", lambda e: e.memset(mx[:, :], 0.0), writes=("mx",))
        P.op("vector",
             lambda e: e.tensor_scalar(lamt[0:1, 1:2], lamt[0:1, 0:1], -1.0, 1.0,
                                       ALU.mult, ALU.add),
             reads=("lamt",), writes=("lamt",))
        P.op("vector",
             lambda e: e.tensor_scalar(lamt[0:1, 2:3], lamt[0:1, 0:1], -1.0, None,
                                       ALU.mult),
             reads=("lamt",), writes=("lamt",))
        P.op("vector", lambda e: e.memset(lamt[0:1, 3:4], 0.0),
             reads=("lamt",), writes=("lamt",))
        bcast(lamt[0:1, 0:4], 4, lamv[:, 0:4], "lamv", "lamt")
        P.op("vector",
             lambda e: e.tensor_scalar(gp8[:, :], g8[:, :], lamv[:, 1:2], None, ALU.mult),
             reads=("g8", "lamv"), writes=("gp8",))
        P.op("vector",
             lambda e: e.tensor_scalar(bp8[:, :], b8[:, :], lamv[:, 1:2], None, ALU.mult),
             reads=("b8", "lamv"), writes=("bp8",))

        for ci in range(CCH):
            for jh in range(2):
                dma("sync", ldb[:, :],
                    d_xT[ci * 128:(ci + 1) * 128, jh * 1024:(jh + 1) * 1024],
                    ("ldb",), ("ldb",), sL0)
                P.op("vector",
                     lambda e, ci=ci, jh=jh: e.tensor_copy(
                         xbf[:, ci * T + jh * 1024: ci * T + (jh + 1) * 1024],
                         ldb[:, :]),
                     reads=("ldb",), writes=(f"xbf{ci}",))

        def load_weight(dram):
            for ci in range(CCH):
                dma("sync", ldb[:, :], dram[ci * 128:(ci + 1) * 128, :],
                    ("ldb",), ("ldb",), sL0)
                P.op("vector",
                     lambda e, ci=ci: e.tensor_copy(
                         wb[:, ci * C:(ci + 1) * C], ldb[:, :]),
                     reads=("ldb",), writes=(f"wb{ci}",))

        load_weight(d_wv)
        for p in range(PAIRS):
            for ci in range(CCH):
                for j in range(4):
                    P.op("tensor",
                         lambda e, ci=ci, j=j, p=p: e.matmul(
                             st[:, j * 512:(j + 1) * 512],
                             wb[:, ci * C + p * 128: ci * C + p * 128 + 128],
                             xbf[:, ci * T + j * 512: ci * T + (j + 1) * 512],
                             start=(ci == 0), stop=(ci == CCH - 1)),
                         reads=(f"wb{ci}", f"xbf{ci}"),
                         writes=("st0", "st1"),
                         sem=sPE)
            P.op("vector", lambda e: e.tensor_copy(eub[:, :], st[:, :]),
                 reads=("st0", "st1"), writes=("eub",))
            for k in range(SC):
                pA, pkey = (pAE, "pAE") if (k % 2 == 0) else (pAO, "pAO")
                P.op("tensor",
                     lambda e, k=k, pA=pA: e.transpose(
                         pA[:, 0:128], eub[:, k * 128:(k + 1) * 128], ident[:, :]),
                     reads=("eub", "ident"), writes=(pkey,), sem=sPE)
                vo = (p * SC + k) * VW
                P.op("vector",
                     lambda e, vo=vo, pA=pA: e.tensor_copy(
                         vaug[:, vo:vo + 64], pA[:, 0:64]),
                     reads=(pkey,), writes=("vaug",))
                P.op("vector",
                     lambda e, vo=vo, pA=pA: e.tensor_copy(
                         vaug[:, vo + 65:vo + 129], pA[:, 64:128]),
                     reads=(pkey,), writes=("vaug",))

        def stats_rows(src, src_keys, n, ncch, col0):
            nj = (n + 511) // 512
            for ci in range(ncch):
                P.op("vector",
                     lambda e, ci=ci: e.tensor_copy(
                         sbb[:, 0:n], src[:, ci * n + col0: ci * n + col0 + n]),
                     reads=(src_keys[ci],), writes=("sbb",))
                P.op("vector",
                     lambda e: e.tensor_mul(sqq[:, 0:n], sbb[:, 0:n], sbb[:, 0:n]),
                     reads=("sbb",), writes=("sqq",))
                for j in range(nj):
                    w = min(512, n - j * 512)
                    P.op("tensor",
                         lambda e, j=j, w=w, ci=ci: e.matmul(
                             pAE[:, j * 512:j * 512 + w], onesb[:, :],
                             sbb[:, j * 512:j * 512 + w],
                             start=(ci == 0), stop=(ci == ncch - 1)),
                         reads=("onesb", "sbb"),
                         writes=("pAE",),
                         sem=sPE)
                    P.op("tensor",
                         lambda e, j=j, w=w, ci=ci: e.matmul(
                             pAO[:, j * 512:j * 512 + w], onesb[:, :],
                             sqq[:, j * 512:j * 512 + w],
                             start=(ci == 0), stop=(ci == ncch - 1)),
                         reads=("onesb", "sqq"),
                         writes=("pAO",),
                         sem=sPE)
            P.op("vector", lambda e: e.tensor_scalar(
                slot(0, 0, n), pAE[0:1, 0:n], 1.0 / C, None, ALU.mult),
                reads=("pAE",), writes=("rows",))
            P.op("vector", lambda e: e.tensor_scalar(
                slot(0, 1, n), pAO[0:1, 0:n], 1.0 / C, None, ALU.mult),
                reads=("pAO",), writes=("rows",))
            P.op("vector", lambda e: e.tensor_mul(
                slot(32, 0, n), slot(0, 0, n), slot(0, 0, n)),
                reads=("rows",), writes=("rows",))
            P.op("vector", lambda e: e.tensor_copy(
                slot(32, 1, n), slot(0, 1, n)),
                reads=("rows",), writes=("rows",))
            P.op("vector", lambda e: e.tensor_sub(
                slot(32, 1, n), slot(32, 1, n), slot(32, 0, n)),
                reads=("rows",), writes=("rows",))
            P.op("scalar", lambda e: e.activation(
                slot(32, 0, n), slot(32, 1, n), AF.Ln,
                bias=epsT[0:1, 0:1], scale=1.0),
                reads=("rows", "epsT"), writes=("rows",))
            P.op("scalar", lambda e: e.activation(
                slot(64, 0, n), slot(32, 0, n), AF.Exp, bias=0.0, scale=-0.5),
                reads=("rows",), writes=("rows",))
            P.op("vector", lambda e: e.tensor_copy(
                slot(64, 1, n), slot(0, 0, n)),
                reads=("rows",), writes=("rows",))
            P.op("vector", lambda e: e.tensor_mul(
                slot(64, 1, n), slot(64, 0, n), slot(64, 1, n)),
                reads=("rows",), writes=("rows",))
            bcast(rows[64:65, 0:2 * n], 2 * n, bcb[:, 0:2 * n], "bcb", "rows")

        for qb in range(NQB):
            q0 = qb * QB
            obkeys = [f"ob{ci}" for ci in range(CCH)]
            for br in range(2):
                load_weight(d_wk1 if br == 0 else d_wk2)
                for p in range(PAIRS):
                    for ci in range(CCH):
                        for j in range(4):
                            P.op("tensor",
                                 lambda e, j=j, ci=ci, p=p: e.matmul(
                                     st[:, j * 512:(j + 1) * 512],
                                     wb[:, ci * C + p * 128: ci * C + p * 128 + 128],
                                     xbf[:, ci * T + j * 512: ci * T + (j + 1) * 512],
                                     start=(ci == 0), stop=(ci == CCH - 1)),
                                 reads=(f"wb{ci}", f"xbf{ci}"),
                                 writes=("st0", "st1"),
                                 sem=sPE)
                    P.op("vector",
                         lambda e, p=p: e.tensor_copy(
                             kb[:, p * T:(p + 1) * T], st[:, :]),
                         reads=("st0", "st1"), writes=(f"kb{p}",))
                load_weight(d_wq1 if br == 0 else d_wq2)
                pending_norm = [None]

                def flush_norm():
                    if pending_norm[0] is not None:
                        pending_norm[0]()
                        pending_norm[0] = None

                for p in range(PAIRS):
                    for ci in range(CCH):
                        for j in range(2):
                            P.op("tensor",
                                 lambda e, j=j, ci=ci, p=p, q0=q0: e.matmul(
                                     st[:, j * 512:(j + 1) * 512],
                                     wb[:, ci * C + p * 128: ci * C + p * 128 + 128],
                                     xbf[:, ci * T + q0 + j * 512:
                                         ci * T + q0 + (j + 1) * 512],
                                     start=(ci == 0), stop=(ci == CCH - 1)),
                                 reads=(f"wb{ci}", f"xbf{ci}"),
                                 writes=("st0",),
                                 sem=sPE)
                    P.op("vector",
                         lambda e: e.tensor_copy(qt[:, :], st[:, 0:QB]),
                         reads=("st0",), writes=("qt",))
                    flush_norm()
                    def emit_scores(h, sc):
                        ko = p * T + sc * 128
                        c0 = h * 1024
                        pb = 0 if h == 0 else 64
                        skey = "st0" if h == 0 else "st1"
                        for j in range(2):
                            P.op("tensor",
                                 lambda e, j=j, ko=ko, c0=c0, pb=pb: e.matmul(
                                     st[:, c0 + j * 512:c0 + (j + 1) * 512],
                                     kb[pb:pb + 64, ko:ko + 128],
                                     qt[pb:pb + 64, j * 512:(j + 1) * 512],
                                     start=True, stop=True,
                                     tile_position=(pb, 0)),
                                 reads=(f"kb{p}", "qt"),
                                 writes=(skey,),
                                 sem=sPE)

                    def emit_av(h, sc):
                        vo = (p * SC + sc) * VW + (0 if h == 0 else 65)
                        c0 = h * 1024
                        pA, pkey = (pAE, "pAE") if h == 0 else (pAO, "pAO")
                        ekey = "etE" if h == 0 else "etO"
                        for j in range(2):
                            P.op("tensor",
                                 lambda e, j=j, vo=vo, c0=c0, pA=pA, sc=sc: e.matmul(
                                     pA[:, j * 512:(j + 1) * 512],
                                     vaug[:, vo:vo + 128],
                                     etb[:, c0 + j * 512:c0 + (j + 1) * 512],
                                     start=(sc == 0), stop=(sc == SC - 1)),
                                 reads=("vaug", ekey),
                                 writes=(pkey,),
                                 sem=sPE)

                    def emit_exp(h, sc):
                        c0 = h * 1024
                        skey = "st0" if h == 0 else "st1"
                        ekey = "etE" if h == 0 else "etO"
                        P.op("scalar",
                             lambda e, c0=c0: e.activation(
                                 etb[:, c0:c0 + 1024], st[:, c0:c0 + 1024],
                                 AF.Exp, bias=0.0, scale=0.125),
                             reads=(skey,), writes=(ekey,))

                    prev = None
                    for sc in range(SC):
                        for h in (0, 1):
                            emit_scores(h, sc)
                            if prev is not None:
                                emit_av(*prev)
                            emit_exp(h, sc)
                            prev = (h, sc)
                    emit_av(*prev)
                    P.op("vector",
                         lambda e: e.tensor_copy(eub[0:65, 0:1024], pAE[0:65, :]),
                         reads=("pAE",), writes=("eub",))
                    P.op("vector",
                         lambda e: e.tensor_copy(eub[0:65, 1024:2048], pAO[0:65, :]),
                         reads=("pAO",), writes=("eub",))

                    def norm_tail(p=p, br=br):
                        P.op("vector",
                             lambda e: e.reciprocal(rows[64:65, 0:2048],
                                                    eub[64:65, 0:2048]),
                             reads=("eub",), writes=("rows",))
                        bcast(rows[64:65, 0:2048], 2048, bcb[:, 0:2048],
                              "bcb", "rows")
                        oc = p * QB
                        if br == 0:
                            P.op("vector",
                                 lambda e, oc=oc: e.tensor_mul(
                                     ob[0:64, oc:oc + QB], eub[0:64, 0:1024],
                                     bcb[0:64, 0:1024]),
                                 reads=("eub", "bcb"), writes=(obkeys[p],))
                            P.op("vector",
                                 lambda e, oc=oc: e.tensor_mul(
                                     ob[64:128, oc:oc + QB], eub[0:64, 1024:2048],
                                     bcb[0:64, 1024:2048]),
                                 reads=("eub", "bcb"), writes=(obkeys[p],))
                        else:
                            P.op("vector",
                                 lambda e: e.tensor_mul(
                                     tmp2[0:64, :], eub[0:64, 0:1024],
                                     bcb[0:64, 0:1024]),
                                 reads=("eub", "bcb"), writes=("tmp2",))
                            P.op("vector",
                                 lambda e: e.tensor_mul(
                                     tmp2[64:128, :], eub[0:64, 1024:2048],
                                     bcb[0:64, 1024:2048]),
                                 reads=("eub", "bcb"), writes=("tmp2",))
                            P.op("vector",
                                 lambda e, oc=oc: e.scalar_tensor_tensor(
                                     ob[:, oc:oc + QB], tmp2[:, :], lamv[:, 2:3],
                                     ob[:, oc:oc + QB], ALU.mult, ALU.add),
                                 reads=("tmp2", "lamv", obkeys[p]),
                                 writes=(obkeys[p],))

                    pending_norm[0] = norm_tail
                flush_norm()

            stats_rows(ob, obkeys, QB, CCH, 0)
            for ci in range(CCH):
                dma("sync", ldb[:, :], d_xT[ci * 128:(ci + 1) * 128, q0:q0 + QB],
                    ("ldb",), ("ldb",), sL0)
                oc = ci * QB
                P.op("vector",
                     lambda e, oc=oc: e.tensor_mul(
                         tmp2[:, :], ob[:, oc:oc + QB], bcb[:, 0:QB]),
                     reads=(obkeys[ci], "bcb"), writes=("tmp2",))
                P.op("vector",
                     lambda e: e.tensor_sub(tmp2[:, :], tmp2[:, :], bcb[:, QB:2 * QB]),
                     reads=("tmp2", "bcb"), writes=("tmp2",))
                P.op("vector",
                     lambda e, ci=ci: e.tensor_scalar(
                         tmp2[:, :], tmp2[:, :], gp8[:, ci:ci + 1],
                         bp8[:, ci:ci + 1], ALU.mult, ALU.add),
                     reads=("tmp2", "gp8", "bp8"), writes=("tmp2",))
                P.op("vector",
                     lambda e, oc=oc: e.tensor_add(
                         ob[:, oc:oc + QB], tmp2[:, :], ldb[:, :]),
                     reads=("tmp2", "ldb"), writes=(obkeys[ci], "ldb")),
            stats_rows(ob, obkeys, QB, CCH, 0)
            for ci in range(CCH):
                oc = ci * QB
                P.op("vector",
                     lambda e, oc=oc: e.tensor_mul(
                         tmp2[:, :], ob[:, oc:oc + QB], bcb[:, 0:QB]),
                     reads=(obkeys[ci], "bcb"), writes=("tmp2",))
                P.op("vector",
                     lambda e, oc=oc: e.tensor_sub(
                         ob[:, oc:oc + QB], tmp2[:, :], bcb[:, QB:2 * QB]),
                     reads=("tmp2", "bcb"), writes=(obkeys[ci],))
                P.op("vector",
                     lambda e, oc=oc: e.reduce_max(
                         mxt[:, 0:1], ob[:, oc:oc + QB], AX.X,
                         apply_absolute_value=True),
                     reads=(obkeys[ci],), writes=("mxt",))
                P.op("vector",
                     lambda e: e.tensor_max(mx[:, 0:1], mx[:, 0:1], mxt[:, 0:1]),
                     reads=("mxt", "mx"), writes=("mx",))
            dma("gpsimd",
                d_uscr[:, q0:q0 + QB].rearrange("(c p) q -> p c q", p=128),
                ob[:, :].rearrange("p (c q) -> p c q", c=CCH),
                tuple(obkeys), (f"uscr{qb}",), sST)

        P.op("tensor",
             lambda e: e.transpose(pAE[0:1, 0:128], mx[:, 0:1], ident[:, :]),
             reads=("mx", "ident"), writes=("pAE",), sem=sPE)
        P.op("vector", lambda e: e.memset(rows[0:1, 0:4], 0.0), writes=("rows",))
        P.op("vector",
             lambda e: e.reduce_max(rows[0:1, 0:1], pAE[0:1, 0:128], AX.X),
             reads=("pAE",), writes=("rows",))
        dma("gpsimd", d_ccin, rows[0:1, 0:4], ("rows",), ("d_ccin",), sCI)
        P.op("gpsimd",
             lambda e: e.collective_compute(
                 "AllReduce", ALU.max, replica_groups=[list(range(8))],
                 ins=[d_ccin], outs=[d_ccout]),
             reads=("d_ccin",), writes=("d_ccout",), sem=sCC, n=1)
        dma("gpsimd", rows[0:1, 4:8], d_ccout, ("d_ccout",), ("rows",), sCI)
        P.op("vector",
             lambda e: e.tensor_scalar(rows[0:1, 0:1], rows[0:1, 4:5],
                                       1.0 / 127.0, EPS, ALU.mult, ALU.max),
             reads=("rows",), writes=("rows",))
        P.op("vector",
             lambda e: e.reciprocal(rows[0:1, 1:2], rows[0:1, 0:1]),
             reads=("rows",), writes=("rows",))
        bcast(rows[0:1, 0:2], 2, xsin[:, 0:2], "xsin", "rows")
        P.op("vector",
             lambda e: e.tensor_scalar(scl8[:, :], ws8[:, :], xsin[:, 0:1],
                                       None, ALU.mult),
             reads=("ws8", "xsin"), writes=("scl8",))
        dma("sync", wt8v.rearrange("p (c n) -> p c n", c=CCH),
            d_wt8.rearrange("(c p) n -> p c n", p=128), ("kb0", "kb1"),
            ("wt8",), sWT)
        for ci in range(CCH):
            P.op("vector",
                 lambda e, ci=ci: e.tensor_copy(
                     wb[:, ci * C:(ci + 1) * C],
                     wt8v[:, ci * C:(ci + 1) * C]),
                 reads=("wt8",), writes=(f"wb{ci}",))
        for ci in range(CCH):
            for jh in range(2):
                dma("sync", ldb[:, :],
                    d_uscr[ci * 128:(ci + 1) * 128, jh * 1024:(jh + 1) * 1024],
                    ("uscr0", "uscr1", "ldb"), ("ldb",), sL0)
                P.op("vector",
                     lambda e: e.tensor_scalar(
                         bcb[:, 0:1024], ldb[:, :], xsin[:, 1:2], MAGIC,
                         ALU.mult, ALU.add),
                     reads=("ldb", "xsin"), writes=("bcb",))
                P.op("vector",
                     lambda e, ci=ci, jh=jh: e.tensor_scalar(
                         xq[:, ci * T + jh * 1024:ci * T + (jh + 1) * 1024],
                         bcb[:, 0:1024], MAGIC, None, ALU.subtract),
                     reads=("bcb",), writes=(f"xq{ci}", f"xbf{ci}", "ldb"))
        for tb in range(4):
            t0 = tb * 512
            dma("sync", utb.rearrange("p (c q) -> p c q", c=CCH),
                d_uscr[:, t0:t0 + 512].rearrange("(c p) q -> p c q", p=128),
                ("uscr0", "uscr1"),
                ("utb", "kb2", "kb3", "kb4", "kb5"), sUT)
            fkeys = [f"fob{nc_}" for nc_ in range(CCH)]
            for nc_ in range(CCH):
                fo = 0 if (nc_ % 2 == 0) else 1024
                fkey = "st0" if (nc_ % 2 == 0) else "st1"
                for ci in range(CCH):
                    P.op("tensor",
                         lambda e, ci=ci, nc_=nc_, fo=fo, t0=t0: e.matmul(
                             st[:, fo:fo + 512],
                             wb[:, ci * C + nc_ * 128: ci * C + nc_ * 128 + 512 - 384],
                             xq[:, ci * T + t0: ci * T + t0 + 512],
                             start=(ci == 0), stop=(ci == CCH - 1)),
                         reads=(f"wb{ci}", f"xq{ci}"),
                         writes=(fkey,),
                         sem=sPE)
                P.op("vector",
                     lambda e, nc_=nc_, fo=fo: e.tensor_scalar(
                         tmp2[:, 0:512], st[:, fo:fo + 512], scl8[:, nc_:nc_ + 1],
                         fb8[:, nc_:nc_ + 1], ALU.mult, ALU.add),
                     reads=(fkey, "scl8", "fb8"), writes=("tmp2",))
                P.op("vector",
                     lambda e, nc_=nc_: e.tensor_add(
                         fob[:, nc_ * 512:(nc_ + 1) * 512], tmp2[:, 0:512],
                         utb[:, nc_ * 512:(nc_ + 1) * 512]),
                     reads=("tmp2", "utb"),
                     writes=(fkeys[nc_], f"ob{nc_ // 2}"))
            stats_rows(fob, fkeys, 512, CCH, 0)
            for nc_ in range(CCH):
                oc = nc_ * 512
                P.op("vector",
                     lambda e, oc=oc: e.tensor_mul(
                         tmp2[:, 0:512], fob[:, oc:oc + 512], bcb[:, 0:512]),
                     reads=(fkeys[nc_], "bcb"), writes=("tmp2",))
                P.op("vector",
                     lambda e, oc=oc: e.tensor_sub(
                         fob[:, oc:oc + 512], tmp2[:, 0:512], bcb[:, 512:1024]),
                     reads=("tmp2", "bcb"), writes=(fkeys[nc_],))
            dma("gpsimd",
                d_yT[:, t0:t0 + 512].rearrange("(c p) q -> p c q", p=128),
                fob[:, :].rearrange("p (c q) -> p c q", c=CCH),
                tuple(fkeys), (f"y{tb}",), sST)

        v = sST.v
        P.ops["gpsimd"].append(lambda e, v=v: e.wait_ge(sST.h, v))

        with nc.Block() as block:
            @block.sync
            def _(eng):
                for f in P.ops["sync"]:
                    f(eng)

            @block.tensor
            def _(eng):
                for f in P.ops["tensor"]:
                    f(eng)

            @block.scalar
            def _(eng):
                for f in P.ops["scalar"]:
                    f(eng)

            @block.vector
            def _(eng):
                for f in P.ops["vector"]:
                    f(eng)

            @block.gpsimd
            def _(eng):
                for f in P.ops["gpsimd"]:
                    f(eng)

    return nc


_NC_CACHE = {}


def kernel(L0_attn_x, L0_attn_wq1, L0_attn_wk1, L0_attn_wq2, L0_attn_wk2,
           L0_attn_wv, L0_attn_ln_gamma, L0_attn_ln_beta, L0_attn_lamb,
           L0_ffn_weight_int8, L0_ffn_weight_scale, L0_ffn_bias):
    f32 = np.float32
    x = np.asarray(L0_attn_x, f32)
    B = x.shape[0]

    def headmajor(w):
        return np.ascontiguousarray(
            np.asarray(w, f32).transpose(1, 0, 2).reshape(C, C))

    wq1 = headmajor(L0_attn_wq1)
    wk1 = headmajor(L0_attn_wk1)
    wq2 = headmajor(L0_attn_wq2)
    wk2 = headmajor(L0_attn_wk2)
    wv = headmajor(L0_attn_wv)
    wt8 = np.ascontiguousarray(np.asarray(L0_ffn_weight_int8, np.int8).T)
    g8 = np.ascontiguousarray(np.asarray(L0_attn_ln_gamma, f32).reshape(CCH, 128).T)
    b8 = np.ascontiguousarray(np.asarray(L0_attn_ln_beta, f32).reshape(CCH, 128).T)
    ws8 = np.ascontiguousarray(
        np.asarray(L0_ffn_weight_scale, f32).reshape(CCH, 128).T)
    fb8 = np.ascontiguousarray(np.asarray(L0_ffn_bias, f32).reshape(CCH, 128).T)
    lam = np.asarray(L0_attn_lamb, f32).reshape(1, 1)
    ident = np.eye(128, dtype=f32)

    if "nc" not in _NC_CACHE:
        _NC_CACHE["nc"] = build_nc()
    nc = _NC_CACHE["nc"]

    in_maps = []
    for b in range(B):
        in_maps.append(dict(
            xT=np.ascontiguousarray(x[b].T), wq1=wq1, wk1=wk1, wq2=wq2,
            wk2=wk2, wv=wv, wt8=wt8, g8=g8, b8=b8, ws8=ws8, fb8=fb8,
            lam=lam, ident=ident))
    res = run_bass_kernel_spmd(nc, in_maps, core_ids=list(range(B)))
    out = np.stack([np.asarray(res.results[b]["yT"], f32).T for b in range(B)],
                   axis=0)
    return out
